# version 04
# baseline (speedup 1.0000x reference)
import numpy as np
from contextlib import ExitStack

import concourse.bass as bass
import concourse.mybir as mybir
from concourse.bass_utils import run_bass_kernel_spmd

F32 = mybir.dt.float32
BF16 = mybir.dt.bfloat16
AF = mybir.ActivationFunctionType
ALU = mybir.AluOpType
AX = mybir.AxisListType

D = 1024
DC = 8
H = 4096
HC = 32
T = 2048
NB = 4
TB = 512
NCORES = 8
EPS = 1e-5
CLAMP = 1.0e6


def build_nc():
    nc = bass.Bass("TRN2", target_bir_lowering=False, debug=False)
    es = ExitStack()

    def din(name, shape, dt=F32):
        return nc.dram_tensor(name, shape, dt, kind="ExternalInput").ap()

    u_d = din("u", [D, T])
    w1t_d = din("w1t", [D, H])
    w2t_d = din("w2t", [H, D])
    w1n_d = din("w1n", [H, D])
    w2n_d = din("w2n", [D, H])
    bt_d = din("bt", [D, D])
    ct_d = din("ct", [D, D])
    dt_d = din("dt", [D, D])
    b1_d = din("b1r", [128, HC])
    b2_d = din("b2r", [128, DC])
    ad_d = din("adr", [128, DC])
    h0_d = din("h0r", [128, DC])
    gam_d = din("gamr", [128, DC])
    bet_d = din("betr", [128, DC])
    msk_d = din("msk", [128, 192])
    hm_d = din("hm", [128, 32])

    out_d = nc.dram_tensor("out", [D, T], F32, kind="ExternalOutput").ap()

    s1_d = nc.dram_tensor("s1sc", [DC, HC, 128, 128], BF16).ap()
    s2_d = nc.dram_tensor("s2sc", [HC, DC, 128, 128], BF16).ap()
    btb_d = nc.dram_tensor("btsc", [DC, DC, 128, 128], BF16).ap()
    ctb_d = nc.dram_tensor("ctsc", [DC, DC, 128, 128], BF16).ap()
    dtb_d = nc.dram_tensor("dtsc", [DC, DC, 128, 128], BF16).ap()
    z2_d = nc.dram_tensor("z2sc", [NB, 128, DC * TB], BF16).ap()
    xs_d = nc.dram_tensor("xssc", [NB, 128, DC * TB], BF16).ap()
    cc_in = nc.dram_tensor("cc_in", [D], F32)
    cc_out = nc.dram_tensor("cc_out", [NCORES, D], F32)

    def sb(name, shape, dt=F32):
        return es.enter_context(nc.sbuf_tensor(name, shape, dt))

    x_t = sb("x", [128, DC * T])
    zb_t = sb("zbt", [128, DC * TB // 2])
    h_t = sb("ht", [128, HC * TB // 2])
    sA_t = sb("sAt", [128, DC * TB // 2])
    sB_t = sb("sBt", [128, DC * TB // 2])
    scrF_t = sb("scrF", [128, DC * TB])
    bu_t = sb("bu", [128, DC * TB])
    xs_t = sb("xs", [128, TB])
    xb_t = sb("xbt", [128, DC * TB // 2])
    s1b_t = sb("s1b", [128, DC * 128])
    s2b_t = sb("s2b", [128, HC * 128 // 2])
    bcd_t = sb("bcdt", [128, 2 * DC * 128])
    apw_t = sb("apw", [128, TB])
    zz_t = sb("zz", [128, TB])
    rows_t = sb("rows", [1, 5 * TB])
    rbf_t = sb("rbf", [1, 4 * TB], BF16)
    small_t = sb("small", [128, 160])
    aw1_t = sb("aw1", [128, HC])
    b1_t = sb("b1s", [128, HC])
    b2_t = sb("b2s", [128, DC])
    gam_t = sb("gams", [128, DC])
    bet_t = sb("bets", [128, DC])
    msk_t = sb("msks", [128, 192])
    hm_t = sb("hms", [128, 32])
    fg_t = sb("fg", [128, 64])
    onec_t = sb("onec", [128, 1], BF16)
    oner_t = sb("oner", [1, 128], BF16)

    zb = zb_t[:, :].bitcast(BF16)
    hb = h_t[:, :].bitcast(BF16)
    sA = sA_t[:, :].bitcast(BF16)
    sB = sB_t[:, :].bitcast(BF16)
    xb = xb_t[:, :].bitcast(BF16)
    s1b = s1b_t[:, :].bitcast(BF16)
    s2b = s2b_t[:, :].bitcast(BF16)
    bcd = bcd_t[:, :].bitcast(BF16)

    stage = h_t
    sgnv = scrF_t[:, :].bitcast(BF16)

    SM = {}
    _off = [0]
    for nm, w in [("a", DC), ("A512", DC), ("A1024", DC), ("A1536", DC), ("A2048", DC),
                  ("P2", DC), ("P3", DC), ("hin", DC), ("hcoef", DC), ("tmp8", DC),
                  ("stcol", DC), ("hsc", NB * DC), ("aw2", DC), ("onef", 1)]:
        SM[nm] = small_t[:, _off[0]:_off[0] + w]
        _off[0] += w

    ps_a = es.enter_context(nc.psum_tensor("ps_a", [128, TB], F32))
    ps_b = es.enter_context(nc.psum_tensor("ps_b", [128, TB], F32))
    ps_s1 = es.enter_context(nc.psum_tensor("ps_s1", [128, TB], F32))
    ps_s2 = es.enter_context(nc.psum_tensor("ps_s2", [128, TB], F32))
    ps_rb = es.enter_context(nc.psum_tensor("ps_rb", [128, TB], F32))
    ps_qb = es.enter_context(nc.psum_tensor("ps_qb", [128, TB], F32))
    ps_mm = [ps_a, ps_b]

    sems = {}
    DMA_SEMS = ["yin", "ysg0", "ysg1", "ywsa", "ywsb", "ws1a", "ws1b", "ws2a", "ws2b",
                "wbca", "wbcb", "wbda", "wbdb", "yzsz", "yzsx", "yzlz", "yzlx", "yout", "gd"]
    TOT_SEMS = {"yin", "yout"}
    for nm in ["t", "s", "v", "cc"] + DMA_SEMS:
        sems[nm] = es.enter_context(nc.semaphore(nm))

    cnt = {k: 0 for k in sems}
    emits = {k: [] for k in ["t", "s", "v", "g", "y"]}

    def op(eng, fn, inc=None, waits=()):
        ws = {}
        for tok in waits:
            if tok is None:
                continue
            sn, sv = tok
            if sn in ws and (ws[sn] == -1 or (sv != -1 and ws[sn] >= sv)):
                continue
            ws[sn] = sv
        emits[eng].append((tuple(ws.items()), fn, inc))
        if inc is None:
            return None
        cnt[inc] += 16 if inc in DMA_SEMS else 1
        return (inc, cnt[inc])

    def dma(sem, outap, inap, waits=(), tot=None):
        eng = "g" if sem == "gd" else "y"
        q = nc.gpsimd if sem == "gd" else nc.sync
        tok = op(eng, lambda o=outap, i=inap, q=q: q.dma_start(out=o, in_=i),
                 inc=sem, waits=waits)
        if tot is None:
            tot = sem in TOT_SEMS
        return (sem, -1) if tot else tok

    def vop(fn, waits=()):
        return op("v", fn, inc="v", waits=waits)

    def sop(fn, waits=()):
        return op("s", fn, inc="s", waits=waits)

    def top(fn, waits=()):
        return op("t", fn, inc="t", waits=waits)

    def top_sil(fn, waits=()):
        return op("t", fn, None, waits=waits)

    V, S, TE, G = nc.vector, nc.scalar, nc.tensor, nc.gpsimd

    last = {}

    def LU(key):
        return list(last.get(key, []))

    def setLU(key, toks):
        last[key] = [t for t in toks if t is not None]

    def addLU(key, toks):
        last[key] = LU(key) + [t for t in toks if t is not None]

    vop(lambda: V.memset(onec_t[:, :], 1.0))
    vop(lambda: V.memset(oner_t[:, :], 1.0))
    vop(lambda: V.memset(zz_t[:, :], 0.0))
    vop(lambda: V.memset(SM["stcol"], 0.0))
    vop(lambda: V.memset(SM["onef"], 1.0))

    x_load_toks = []
    for k in range(DC):
        x_load_toks.append(dma("yin", x_t[:, k * T:(k + 1) * T],
                               u_d[k * 128:(k + 1) * 128, :]))
    sv_toks = []
    for dst, src in [(b1_t[:, :], b1_d), (b2_t[:, :], b2_d), (gam_t[:, :], gam_d),
                     (bet_t[:, :], bet_d), (msk_t[:, :], msk_d), (hm_t[:, :], hm_d)]:
        sv_toks.append(dma("yin", dst, src))
    tok_ad = dma("yin", SM["tmp8"], ad_d)
    tok_h0 = dma("yin", SM["hcoef"], h0_d)

    t_e = sop(lambda: S.activation(out=SM["a"], in_=SM["tmp8"], func=AF.Exp),
              waits=[tok_ad])
    t_sp = sop(lambda: S.activation(out=SM["a"], in_=SM["a"], func=AF.Ln, bias=1.0),
               waits=[t_e])
    t_a = vop(lambda: V.tensor_scalar_mul(SM["a"], SM["a"], -1.0), waits=[t_sp])
    t_p = vop(lambda: V.tensor_copy(SM["A512"], SM["a"]), waits=[t_a])
    for _ in range(9):
        t_p = vop(lambda: V.tensor_tensor(out=SM["A512"], in0=SM["A512"],
                                          in1=SM["A512"], op=ALU.mult), waits=[t_p])
    t_1024 = vop(lambda: V.tensor_tensor(out=SM["A1024"], in0=SM["A512"], in1=SM["A512"],
                                         op=ALU.mult), waits=[t_p])
    t_1536 = vop(lambda: V.tensor_tensor(out=SM["A1536"], in0=SM["A1024"], in1=SM["A512"],
                                         op=ALU.mult), waits=[t_1024])
    t_2048 = vop(lambda: V.tensor_tensor(out=SM["A2048"], in0=SM["A1024"], in1=SM["A1024"],
                                         op=ALU.mult), waits=[t_1024])
    t_P2 = vop(lambda: V.tensor_tensor(out=SM["P2"], in0=SM["A2048"], in1=SM["A2048"],
                                       op=ALU.mult), waits=[t_2048])
    t_P3 = vop(lambda: V.tensor_tensor(out=SM["P3"], in0=SM["P2"], in1=SM["A2048"],
                                       op=ALU.mult), waits=[t_P2])

    wprep = {"s1": [None] * DC, "s2": [None] * HC,
             "bt": [None] * DC, "ct": [None] * DC, "dt": [None] * DC}

    for k in range(DC):
        q = k % 2
        st_sl = stage[:, q * 4096:(q + 1) * 4096]
        sg_sl = sgnv[:, q * 4096:(q + 1) * 4096]
        tl = dma("ysg%d" % q, st_sl, w1t_d[k * 128:(k + 1) * 128, :], waits=LU(("stage", q)))
        ts = sop(lambda o=sg_sl, i=st_sl: S.activation(out=o, in_=i, func=AF.Sign),
                 waits=[tl] + LU(("sgn", q)))
        setLU(("stage", q), [ts])
        td = dma("ywsa" if q == 0 else "ywsb", s1_d[k].rearrange("c p j -> p c j"),
                 sg_sl.rearrange("p (c j) -> p c j", c=HC, j=128), waits=[ts])
        setLU(("sgn", q), [td])
        wprep["s1"][k] = td

    for g in range(8):
        q = g % 2
        st_sl = stage[:, q * 4096:(q + 1) * 4096]
        tl = dma("ysg%d" % q, st_sl.rearrange("p (c j) -> p c j", c=4, j=D),
                 w1n_d[g * 512:(g + 1) * 512, :].rearrange("(c p) j -> p c j", c=4, p=128),
                 waits=LU(("stage", q)))
        vts = []
        for i in range(4):
            c = g * 4 + i
            vts.append(vop(lambda c=c, sl=st_sl[:, i * D:(i + 1) * D]:
                           V.tensor_reduce(out=aw1_t[:, c:c + 1], in_=sl, axis=AX.X,
                                           op=ALU.add, apply_absolute_value=True),
                           waits=[tl]))
        setLU(("stage", q), vts)

    alpha2_toks = []
    for k in range(DC):
        q = k % 2
        st_sl = stage[:, q * 4096:(q + 1) * 4096]
        tl = dma("ysg%d" % q, st_sl, w2n_d[k * 128:(k + 1) * 128, :], waits=LU(("stage", q)))
        tv = vop(lambda k=k, sl=st_sl: V.tensor_reduce(
            out=SM["aw2"][:, k:k + 1], in_=sl, axis=AX.X, op=ALU.add,
            apply_absolute_value=True), waits=[tl])
        setLU(("stage", q), [tv])
        alpha2_toks.append(tv)

    for g in range(8):
        q = g % 2
        st_sl = stage[:, q * 4096:(q + 1) * 4096]
        sg_sl = sgnv[:, q * 4096:(q + 1) * 4096]
        tl = dma("ysg%d" % q, st_sl.rearrange("p (c j) -> p c j", c=4, j=D),
                 w2t_d[g * 512:(g + 1) * 512, :].rearrange("(c p) j -> p c j", c=4, p=128),
                 waits=LU(("stage", q)))
        ts = sop(lambda o=sg_sl, i=st_sl: S.activation(out=o, in_=i, func=AF.Sign),
                 waits=[tl] + LU(("sgn", q)))
        setLU(("stage", q), [ts])
        tds = []
        for i in range(4):
            k = g * 4 + i
            td = dma("ywsa" if q == 0 else "ywsb", s2_d[k].rearrange("c p j -> p c j"),
                     sg_sl[:, i * D:(i + 1) * D].rearrange("p (c j) -> p c j", c=DC, j=128),
                     waits=[ts])
            tds.append(td)
            wprep["s2"][k] = td
        setLU(("sgn", q), tds)

    for name, src_d, dst_d in [("bt", bt_d, btb_d), ("ct", ct_d, ctb_d),
                               ("dt", dt_d, dtb_d)]:
        for k in range(DC):
            q = k % 2
            st_sl = stage[:, q * 4096:q * 4096 + D]
            sg_sl = sgnv[:, q * 4096:q * 4096 + D]
            tl = dma("ysg%d" % q, st_sl, src_d[k * 128:(k + 1) * 128, :], waits=LU(("stage", q)))
            tv = vop(lambda o=sg_sl, i=st_sl: V.tensor_copy(o, i),
                     waits=[tl] + LU(("sgn", q)))
            setLU(("stage", q), [tv])
            td = dma("ywsa" if q == 0 else "ywsb", dst_d[k].rearrange("c p j -> p c j"),
                     sg_sl.rearrange("p (c j) -> p c j", c=DC, j=128), waits=[tv])
            setLU(("sgn", q), [td])
            wprep[name][k] = td

    t_aw1 = vop(lambda: V.tensor_scalar_mul(aw1_t[:, :], aw1_t[:, :], 1.0 / D))
    t_aw2 = vop(lambda: V.tensor_scalar_mul(SM["aw2"], SM["aw2"], 0.5 / H),
                waits=alpha2_toks)
    t_b2h = vop(lambda: V.tensor_scalar_mul(b2_t[:, :], b2_t[:, :], 0.5), waits=sv_toks)

    setLU("hb_init", LU(("stage", 0)) + LU(("stage", 1)))
    for k in range(DC):
        setLU(("scrF", k), LU(("sgn", 0)) + LU(("sgn", 1)))

    ROW = {nm: rows_t[0:1, i * TB:(i + 1) * TB] for i, nm in
           enumerate(["m", "v", "t", "r", "q"])}
    RBF = {nm: rbf_t[0:1, i * TB:(i + 1) * TB] for i, nm in
           enumerate(["rhi", "rlo", "qhi", "qlo"])}

    def ln_stats_bcast(src_slices, src_is_bf, extra_cast_waits):
        cast_toks = []
        for k in range(DC):
            sl = src_slices[k]
            dstA = sA[:, k * TB:(k + 1) * TB]
            dstB = sB[:, k * TB:(k + 1) * TB]
            if src_is_bf:
                cast_toks.append(vop(lambda o=dstB, i=sl: V.tensor_tensor(
                    out=o, in0=i, in1=i, op=ALU.mult), waits=extra_cast_waits))
            else:
                cast_toks.append(vop(lambda o=dstA, i=sl: V.tensor_copy(o, i),
                                     waits=extra_cast_waits))
                cast_toks.append(vop(lambda o=dstB, i=sl: V.tensor_tensor(
                    out=o, in0=i, in1=i, op=ALU.mult), waits=extra_cast_waits))
        srcA = src_slices if src_is_bf else [sA[:, k * TB:(k + 1) * TB] for k in range(DC)]
        w_ps = LU("ps_s") + cast_toks
        tok_s1 = None
        for k in range(DC):
            f = (lambda k=k, s=srcA[k]: TE.matmul(ps_s1[0:1, :], onec_t[:, :], s,
                                                  start=(k == 0), stop=(k == DC - 1)))
            if k == DC - 1:
                tok_s1 = top(f)
            else:
                top_sil(f, waits=w_ps if k == 0 else ())
        tok_s2 = None
        for k in range(DC):
            f = (lambda k=k, s=sB[:, k * TB:(k + 1) * TB]: TE.matmul(
                ps_s2[0:1, :], onec_t[:, :], s, start=(k == 0), stop=(k == DC - 1)))
            if k == DC - 1:
                tok_s2 = top(f)
            else:
                top_sil(f)
        wr = LU("rows")
        t1 = vop(lambda: V.tensor_scalar_mul(ROW["m"], ps_s1[0:1, :], 1.0 / D),
                 waits=[tok_s1] + wr)
        t2 = vop(lambda: V.tensor_scalar(out=ROW["v"], in0=ps_s2[0:1, :], scalar1=1.0 / D,
                                         scalar2=EPS, op0=ALU.mult, op1=ALU.add),
                 waits=[tok_s2] + wr)
        vop(lambda: V.tensor_tensor(out=ROW["t"], in0=ROW["m"], in1=ROW["m"],
                                    op=ALU.mult), waits=[t1])
        t4 = vop(lambda: V.tensor_tensor(out=ROW["v"], in0=ROW["v"], in1=ROW["t"],
                                         op=ALU.subtract), waits=[t2])
        setLU("ps_s", [t1, t2])
        t5a = vop(lambda: V.reciprocal(out=ROW["t"], in_=ROW["v"]), waits=[t4])
        t5 = sop(lambda: S.activation(out=ROW["r"], in_=ROW["t"], func=AF.Sqrt),
                 waits=[t5a])
        t6 = vop(lambda: V.tensor_tensor(out=ROW["t"], in0=ROW["r"], in1=ROW["r"],
                                         op=ALU.mult), waits=[t5])
        vop(lambda: V.tensor_tensor(out=ROW["t"], in0=ROW["t"], in1=ROW["v"],
                                    op=ALU.mult), waits=[t6])
        vop(lambda: V.tensor_scalar(out=ROW["t"], in0=ROW["t"], scalar1=-0.5,
                                    scalar2=1.5, op0=ALU.mult, op1=ALU.add))
        t9 = vop(lambda: V.tensor_tensor(out=ROW["r"], in0=ROW["r"], in1=ROW["t"],
                                         op=ALU.mult))
        tq = vop(lambda: V.scalar_tensor_tensor(out=ROW["q"], in0=ROW["m"], scalar=-1.0,
                                                in1=ROW["r"], op0=ALU.mult, op1=ALU.mult),
                 waits=[t9])
        wb = LU("rbf")
        c1 = vop(lambda: V.tensor_copy(RBF["rhi"], ROW["r"]), waits=[t9] + wb)
        c2 = vop(lambda: V.tensor_tensor(out=RBF["rlo"], in0=ROW["r"], in1=RBF["rhi"],
                                         op=ALU.subtract), waits=[c1])
        c3 = vop(lambda: V.tensor_copy(RBF["qhi"], ROW["q"]), waits=[tq] + wb)
        c4 = vop(lambda: V.tensor_tensor(out=RBF["qlo"], in0=ROW["q"], in1=RBF["qhi"],
                                         op=ALU.subtract), waits=[c3])
        setLU("rows", [c2, c4])
        top_sil(lambda: TE.matmul(ps_rb[:, :], oner_t[:, :], RBF["rhi"], start=True,
                                  stop=False), waits=[c2] + LU("ps_rb"))
        tok_rb = top(lambda: TE.matmul(ps_rb[:, :], oner_t[:, :], RBF["rlo"], start=False,
                                       stop=True))
        top_sil(lambda: TE.matmul(ps_qb[:, :], oner_t[:, :], RBF["qhi"], start=True,
                                  stop=False), waits=[c4] + LU("ps_qb"))
        tok_qb = top(lambda: TE.matmul(ps_qb[:, :], oner_t[:, :], RBF["qlo"], start=False,
                                       stop=True))
        setLU("rbf", [tok_rb, tok_qb])
        return tok_rb, tok_qb

    def ln_to_z(b, extra_waits):
        xsl = [x_t[:, k * T + b * TB:k * T + (b + 1) * TB] for k in range(DC)]
        tok_rb, tok_qb = ln_stats_bcast(xsl, False, extra_waits)
        z_toks = []
        for k in range(DC):
            t1 = vop(lambda k=k, s=xsl[k]: V.tensor_tensor(
                out=scrF_t[:, k * TB:(k + 1) * TB], in0=s, in1=ps_rb[:, :], op=ALU.mult),
                waits=[tok_rb] + LU(("scrF", k)))
            t2 = vop(lambda k=k: V.tensor_tensor(
                out=zb[:, k * TB:(k + 1) * TB], in0=scrF_t[:, k * TB:(k + 1) * TB],
                in1=ps_qb[:, :], op=ALU.add), waits=[tok_qb] + LU(("zb", k)))
            setLU(("scrF", k), [t2])
            setLU(("zb", k), [t2])
            z_toks.append(t2)
        setLU("ps_rb", z_toks)
        setLU("ps_qb", z_toks)
        return z_toks

    def ffn_block(b, z_toks):
        gelu_toks = []
        zb_read = None
        for c in range(HC):
            q = c % 2
            buf = s1b[:, q * DC * 128:(q + 1) * DC * 128]
            tl = dma("ws1a" if q == 0 else "ws1b",
                     buf.rearrange("p (k j) -> p k j", k=DC, j=128),
                     s1_d[:, c].rearrange("k p j -> p k j"),
                     waits=LU(("s1b", q)) + ([("ywsa", -1), ("ywsb", -1)] if c < 2 else []))
            mm_w = [tl] + LU(("ps", q)) + z_toks
            tok_mm = None
            for k in range(DC):
                f = (lambda k=k, q=q, buf=buf: TE.matmul(
                    ps_mm[q][:, :], buf[:, k * 128:(k + 1) * 128],
                    zb[:, k * TB:(k + 1) * TB], start=(k == 0), stop=(k == DC - 1)))
                if k == DC - 1:
                    tok_mm = top(f)
                else:
                    top_sil(f, waits=mm_w if k == 0 else ())
            setLU(("s1b", q), [tok_mm])
            zb_read = tok_mm
            tg = sop(lambda c=c, q=q: S.activation(
                out=hb[:, c * TB:(c + 1) * TB], in_=ps_mm[q][:, :], func=AF.Gelu,
                scale=aw1_t[:, c:c + 1], bias=b1_t[:, c:c + 1]),
                waits=[tok_mm, t_aw1] + sv_toks + (LU("hb_init") if c == 0 else []))
            gelu_toks.append(tg)
            setLU(("ps", q), [tg])
        setLU("hb_init", [])
        for k in range(DC):
            addLU(("zb", k), [zb_read])
        x_toks = []
        for c in range(DC):
            q = c % 2
            tok_mm = None
            for half in range(2):
                hq = (2 * c + half) % 2
                buf = s2b[:, hq * 16 * 128:(hq + 1) * 16 * 128]
                tl = dma("ws2a" if hq == 0 else "ws2b",
                         buf.rearrange("p (k j) -> p k j", k=16, j=128),
                         s2_d[half * 16:(half + 1) * 16, c].rearrange("k p j -> p k j"),
                         waits=LU(("s2b", hq)) + ([("ywsa", -1), ("ywsb", -1)] if c < 1 else []))
                mm_w = [tl] + (LU(("ps", q)) + gelu_toks if half == 0 else [])
                for k16 in range(16):
                    k = half * 16 + k16
                    f = (lambda k=k, k16=k16, q=q, buf=buf: TE.matmul(
                        ps_mm[q][:, :], buf[:, k16 * 128:(k16 + 1) * 128],
                        hb[:, k * TB:(k + 1) * TB], start=(k == 0), stop=(k == HC - 1)))
                    if k16 == 15:
                        tok_mm = top(f)
                    else:
                        top_sil(f, waits=mm_w if k16 == 0 else ())
                setLU(("s2b", hq), [tok_mm])
            te = sop(lambda c=c, q=q: S.activation(
                out=scrF_t[:, c * TB:(c + 1) * TB], in_=ps_mm[q][:, :], func=AF.Identity,
                scale=SM["aw2"][:, c:c + 1], bias=b2_t[:, c:c + 1]),
                waits=[tok_mm] + LU(("scrF", c)) + [t_aw2, t_b2h])
            tx = vop(lambda c=c, b=b: V.tensor_tensor(
                out=x_t[:, c * T + b * TB:c * T + (b + 1) * TB],
                in0=x_t[:, c * T + b * TB:c * T + (b + 1) * TB],
                in1=scrF_t[:, c * TB:(c + 1) * TB], op=ALU.add), waits=[te])
            setLU(("scrF", c), [tx])
            setLU(("ps", q), [te])
            x_toks.append(tx)
        return x_toks

    scan_store = []
    z2_store = []
    stcol_tok = {k: None for k in range(DC)}
    for b in range(NB):
        zt = ln_to_z(b, x_load_toks)
        xupd = ffn_block(b, zt)
        zt2 = ln_to_z(b, xupd)
        tz = dma("yzsz", z2_d[b], zb[:, :], waits=zt2)
        z2_store.append(tz)
        for k in range(DC):
            addLU(("zb", k), [tz])
        bu_toks = []
        for c in range(DC):
            q = c % 2
            buf = bcd[:, q * DC * 128:(q + 1) * DC * 128]
            tl = dma("wbca" if q == 0 else "wbcb",
                     buf.rearrange("p (k j) -> p k j", k=DC, j=128),
                     btb_d[:, c].rearrange("k p j -> p k j"),
                     waits=LU(("bcd", q)) + ([("ywsa", -1), ("ywsb", -1)] if (b == 0 and c < 2) else []))
            mm_w = [tl] + LU(("ps", q)) + zt2
            tok_mm = None
            for k in range(DC):
                f = (lambda k=k, q=q, buf=buf: TE.matmul(
                    ps_mm[q][:, :], buf[:, k * 128:(k + 1) * 128],
                    zb[:, k * TB:(k + 1) * TB], start=(k == 0), stop=(k == DC - 1)))
                if k == DC - 1:
                    tok_mm = top(f)
                else:
                    top_sil(f, waits=mm_w if k == 0 else ())
            setLU(("bcd", q), [tok_mm])
            for kk in range(DC):
                addLU(("zb", kk), [tok_mm])
            tb_ = sop(lambda c=c, q=q: S.copy(out=bu_t[:, c * TB:(c + 1) * TB],
                                              in_=ps_mm[q][:, :]),
                      waits=[tok_mm] + LU(("bu", c)))
            setLU(("ps", q), [tb_])
            bu_toks.append(tb_)
        xs_cast = []
        for k in range(DC):
            tsc = vop(lambda k=k: V.tensor_tensor_scan(
                out=xs_t[:, :],
                data0=SM["a"][:, k:k + 1].broadcast_to((128, TB)),
                data1=bu_t[:, k * TB:(k + 1) * TB],
                initial=SM["stcol"][:, k:k + 1],
                op0=ALU.mult, op1=ALU.add),
                waits=[bu_toks[k], t_a, stcol_tok[k]] + LU("xs"))
            setLU(("bu", k), [tsc])
            tst = vop(lambda k=k: V.tensor_copy(
                SM["stcol"][:, k:k + 1], xs_t[:, TB - 1:TB]),
                waits=[tsc])
            stcol_tok[k] = tst
            tcb = vop(lambda k=k: V.tensor_copy(xb[:, k * TB:(k + 1) * TB],
                                                xs_t[:, :]),
                      waits=[tsc] + LU(("xb", k)))
            setLU("xs", [tcb])
            setLU(("xb", k), [tcb])
            xs_cast.append(tcb)
        txs = dma("yzsx", xs_d[b], xb[:, :], waits=xs_cast)
        scan_store.append(txs)
        for k in range(DC):
            addLU(("xb", k), [txs])

    cc_dmas = []
    for k in range(DC):
        cc_dmas.append(dma("gd", cc_in.ap()[k * 128:(k + 1) * 128].unsqueeze(1),
                           SM["stcol"][:, k:k + 1],
                           waits=[stcol_tok[kk] for kk in range(DC)], tot=False))
    tok_cc = op("g", lambda: G.collective_compute(
        "AllGather", ALU.bypass, replica_groups=[list(range(NCORES))],
        ins=[cc_in.ap().opt()], outs=[cc_out.ap().opt()]), inc="cc", waits=cc_dmas)
    fg_toks = []
    for k in range(DC):
        for r in range(NCORES):
            fg_toks.append(dma("gd", fg_t[:, k * 8 + r:k * 8 + r + 1],
                               cc_out.ap()[r, k * 128:(k + 1) * 128].unsqueeze(1),
                               waits=[tok_cc], tot=True))

    t_h0mv = vop(lambda: V.tensor_copy(SM["hsc"][:, 0:DC], SM["hcoef"]), waits=[tok_h0])
    h0_ap = SM["hsc"][:, 0:DC]
    fsel_aps = [SM["tmp8"], SM["hcoef"], SM["A1024"]]
    fsel_toks = []
    for i in range(3):
        tm = vop(lambda i=i: V.tensor_tensor(
            out=scrF_t[:, 2048 + i * 64:2048 + (i + 1) * 64], in0=fg_t[:, :],
            in1=msk_t[:, i * 64:(i + 1) * 64], op=ALU.mult),
            waits=fg_toks + sv_toks + LU(("scrF", 4)))
        tr = vop(lambda i=i, d=fsel_aps[i]: V.tensor_reduce(
            out=d, in_=scrF_t[:, 2048 + i * 64:2048 + (i + 1) * 64]
            .rearrange("p (k r) -> p k r", k=8, r=8),
            axis=AX.X, op=ALU.add), waits=[tm])
        fsel_toks.append(tr)
    addLU(("scrF", 4), fsel_toks)
    vop(lambda: V.tensor_copy(SM["hin"], fsel_aps[0]), waits=fsel_toks)
    vop(lambda: V.scalar_tensor_tensor(out=SM["A1536"], in0=fsel_aps[1], scalar=1.0,
                                       in1=SM["A2048"], op0=ALU.mult, op1=ALU.mult))
    vop(lambda: V.tensor_tensor(out=SM["hin"], in0=SM["hin"], in1=SM["A1536"],
                                op=ALU.add))
    vop(lambda: V.tensor_tensor(out=SM["A1536"], in0=fsel_aps[2], in1=SM["P2"],
                                op=ALU.mult))
    vop(lambda: V.tensor_tensor(out=SM["hin"], in0=SM["hin"], in1=SM["A1536"],
                                op=ALU.add))
    vop(lambda: V.tensor_copy(SM["A1536"], hm_t[:, 0:8]))
    vop(lambda: V.tensor_tensor(out=SM["P2"], in0=SM["A2048"], in1=hm_t[:, 8:16],
                                op=ALU.mult))
    vop(lambda: V.tensor_tensor(out=SM["A1536"], in0=SM["A1536"], in1=SM["P2"],
                                op=ALU.add))
    vop(lambda: V.tensor_tensor(out=SM["P2"], in0=SM["A2048"], in1=SM["A2048"],
                                op=ALU.mult))
    vop(lambda: V.tensor_tensor(out=SM["tmp8"], in0=SM["P2"], in1=hm_t[:, 16:24],
                                op=ALU.mult))
    vop(lambda: V.tensor_tensor(out=SM["A1536"], in0=SM["A1536"], in1=SM["tmp8"],
                                op=ALU.add))
    vop(lambda: V.tensor_tensor(out=SM["tmp8"], in0=SM["P3"], in1=hm_t[:, 24:32],
                                op=ALU.mult), waits=[t_P3])
    vop(lambda: V.tensor_tensor(out=SM["A1536"], in0=SM["A1536"], in1=SM["tmp8"],
                                op=ALU.add))
    vop(lambda: V.tensor_tensor(out=SM["tmp8"], in0=SM["A1536"], in1=h0_ap,
                                op=ALU.mult), waits=[t_h0mv])
    th_in = vop(lambda: V.tensor_tensor(out=SM["hin"], in0=SM["hin"], in1=SM["tmp8"],
                                        op=ALU.add))
    ts0 = vop(lambda: V.tensor_copy(SM["hsc"][:, 0:8], SM["hin"]), waits=[th_in])
    ts1 = vop(lambda: V.tensor_tensor(out=SM["hsc"][:, 8:16], in0=SM["hin"],
                                      in1=SM["A512"], op=ALU.mult))
    ts2 = vop(lambda: V.tensor_tensor(out=SM["hsc"][:, 16:24], in0=SM["hsc"][:, 8:16],
                                      in1=SM["A512"], op=ALU.mult))
    ts3 = vop(lambda: V.tensor_tensor(out=SM["hsc"][:, 24:32], in0=SM["hsc"][:, 16:24],
                                      in1=SM["A512"], op=ALU.mult))
    hsc_toks = [ts0, ts1, ts2, ts3]

    final_x = {}
    for b in range(NB):
        tlx = dma("yzlx", xb[:, :], xs_d[b],
                  waits=[("yzsx", -1)] + sum([LU(("xb", k)) for k in range(DC)], []))
        for k in range(DC):
            setLU(("xb", k), [tlx])
        tlz = dma("yzlz", zb[:, :], z2_d[b],
                  waits=[("yzsz", -1)] + sum([LU(("zb", k)) for k in range(DC)], []))
        for k in range(DC):
            setLU(("zb", k), [tlz])
        corr_toks = []
        for k in range(DC):
            ta = vop(lambda k=k: V.tensor_tensor_scan(
                out=apw_t[:, :], data0=SM["a"][:, k:k + 1].broadcast_to((128, TB)),
                data1=zz_t[:, :], initial=SM["onef"][:, 0:1],
                op0=ALU.mult, op1=ALU.add), waits=[t_a] + hsc_toks + LU("apw"))
            tc = vop(lambda k=k, b=b: V.scalar_tensor_tensor(
                out=xb[:, k * TB:(k + 1) * TB], in0=apw_t[:, :],
                scalar=SM["hsc"][:, b * 8 + k:b * 8 + k + 1],
                in1=xb[:, k * TB:(k + 1) * TB], op0=ALU.mult, op1=ALU.add),
                waits=[ta, tlx])
            setLU("apw", [tc])
            setLU(("xb", k), [tc])
            corr_toks.append(tc)
        act_toks = []
        y_read = None
        for c in range(DC):
            q = c % 2
            bufC = bcd[:, q * DC * 128:(q + 1) * DC * 128]
            bufD = bcd[:, (2 + q) * DC * 128:(3 + q) * DC * 128]
            tlc = dma("wbca" if q == 0 else "wbcb",
                      bufC.rearrange("p (k j) -> p k j", k=DC, j=128),
                      ctb_d[:, c].rearrange("k p j -> p k j"),
                      waits=LU(("bcd", q)) + ([("ywsa", -1), ("ywsb", -1)] if (b == 0 and c < 2) else []))
            tld = dma("wbda" if q == 0 else "wbdb",
                      bufD.rearrange("p (k j) -> p k j", k=DC, j=128),
                      dtb_d[:, c].rearrange("k p j -> p k j"),
                      waits=LU(("bcdD", q)) + ([("ywsa", -1), ("ywsb", -1)] if (b == 0 and c < 2) else []))
            mm_w = [tlc, tld, tlz] + LU(("ps", q)) + corr_toks
            for k in range(DC):
                f = (lambda k=k, q=q, bufC=bufC: TE.matmul(
                    ps_mm[q][:, :], bufC[:, k * 128:(k + 1) * 128],
                    xb[:, k * TB:(k + 1) * TB], start=(k == 0), stop=False))
                top_sil(f, waits=mm_w if k == 0 else ())
            tok_mm = None
            for k in range(DC):
                f = (lambda k=k, q=q, bufD=bufD: TE.matmul(
                    ps_mm[q][:, :], bufD[:, k * 128:(k + 1) * 128],
                    zb[:, k * TB:(k + 1) * TB], start=False, stop=(k == DC - 1)))
                if k == DC - 1:
                    tok_mm = top(f)
                else:
                    top_sil(f)
            setLU(("bcd", q), [tok_mm])
            setLU(("bcdD", q), [tok_mm])
            y_read = tok_mm
            tga = sop(lambda c=c, q=q: S.activation(
                out=sA[:, c * TB:(c + 1) * TB], in_=ps_mm[q][:, :], func=AF.Gelu),
                waits=[tok_mm])
            setLU(("ps", q), [tga])
            act_toks.append(tga)
        for k in range(DC):
            addLU(("xb", k), [y_read])
            addLU(("zb", k), [y_read])
        actsl = [sA[:, k * TB:(k + 1) * TB] for k in range(DC)]
        tok_rb, tok_qb = ln_stats_bcast(actsl, True, act_toks)
        res_toks = []
        for k in range(DC):
            t1 = vop(lambda k=k: V.tensor_tensor(
                out=scrF_t[:, k * TB:(k + 1) * TB], in0=sA[:, k * TB:(k + 1) * TB],
                in1=ps_rb[:, :], op=ALU.mult),
                waits=[tok_rb, act_toks[k]] + LU(("scrF", k)))
            vop(lambda k=k: V.tensor_tensor(
                out=scrF_t[:, k * TB:(k + 1) * TB], in0=scrF_t[:, k * TB:(k + 1) * TB],
                in1=ps_qb[:, :], op=ALU.add), waits=[tok_qb])
            t3 = sop(lambda k=k: S.activation(
                out=scrF_t[:, k * TB:(k + 1) * TB], in_=scrF_t[:, k * TB:(k + 1) * TB],
                func=AF.Identity, scale=gam_t[:, k:k + 1], bias=bet_t[:, k:k + 1]),
                waits=[(("v", cnt["v"]))] + sv_toks)
            t4 = vop(lambda k=k: V.tensor_scalar(
                out=scrF_t[:, k * TB:(k + 1) * TB], in0=scrF_t[:, k * TB:(k + 1) * TB],
                scalar1=CLAMP, scalar2=-CLAMP, op0=ALU.min, op1=ALU.max), waits=[t3])
            t5 = vop(lambda k=k, b=b: V.tensor_tensor(
                out=x_t[:, k * T + b * TB:k * T + (b + 1) * TB],
                in0=x_t[:, k * T + b * TB:k * T + (b + 1) * TB],
                in1=scrF_t[:, k * TB:(k + 1) * TB], op=ALU.add), waits=[t4])
            setLU(("scrF", k), [t5])
            res_toks.append(t5)
        setLU("ps_rb", res_toks)
        setLU("ps_qb", res_toks)
        zt4 = ln_to_z(b, res_toks)
        xupd3 = ffn_block(b, zt4)
        for k in range(DC):
            final_x[k] = xupd3[k]

    st_toks = []
    for k in range(DC):
        st_toks.append(dma("yout", out_d[k * 128:(k + 1) * 128, :],
                           x_t[:, k * T:(k + 1) * T], waits=[final_x[k]]))
    op("g", lambda: None, None, waits=st_toks)

    hi_water = {e: {} for e in emits}

    def replay(e, own):
        hw = hi_water[e]
        eng = {"y": nc.sync, "t": nc.tensor, "s": nc.scalar, "v": nc.vector,
               "g": nc.gpsimd}[e]
        kown = 0
        serialize = own in ("s", "v")
        for ws, fn, inc in emits[e]:
            if serialize and kown > 0 and hw.get(own, 0) < kown:
                eng.wait_ge(sems[own], kown)
                hw[own] = kown
            for sn, sv in ws:
                if sn == own and serialize:
                    continue
                if sn == own:
                    continue
                if sv == -1:
                    sv = cnt[sn]
                if hw.get(sn, 0) >= sv:
                    continue
                eng.wait_ge(sems[sn], sv)
                hw[sn] = sv
            ins = fn()
            if inc is not None:
                assert ins is not None, "op expected to return instruction"
                ins.then_inc(sems[inc], 16 if inc in DMA_SEMS else 1)
                if inc == own:
                    kown += 1

    with nc.Block() as block:
        @block.sync
        def _(sync):
            replay("y", None)

        @block.tensor
        def _(te):
            replay("t", "t")

        @block.scalar
        def _(sc):
            replay("s", "s")

        @block.vector
        def _(ve):
            replay("v", "v")

        @block.gpsimd
        def _(gp):
            replay("g", "g")

    es.close()
    return nc


_NC_CACHE = None


def kernel(L0_attn_u, L0_attn_A_diag, L0_attn_Bmat, L0_attn_Cmat, L0_attn_Dmat,
           L0_attn_h0, L0_attn_ln_gamma, L0_attn_ln_beta,
           L0_ffn_w1, L0_ffn_b1, L0_ffn_w2, L0_ffn_b2):
    global _NC_CACHE
    u = np.asarray(L0_attn_u, np.float32)
    w1 = np.ascontiguousarray(np.asarray(L0_ffn_w1, np.float32))
    w2 = np.ascontiguousarray(np.asarray(L0_ffn_w2, np.float32))
    Bm = np.asarray(L0_attn_Bmat, np.float32)
    Cm = np.asarray(L0_attn_Cmat, np.float32)
    Dm = np.asarray(L0_attn_Dmat, np.float32)

    def vr(v, kk):
        return np.ascontiguousarray(np.asarray(v, np.float32).reshape(kk, 128).T)

    def rep(vrow):
        return np.ascontiguousarray(
            np.broadcast_to(vrow[None, :], (128, vrow.shape[0])).astype(np.float32))

    shared = {
        "w1t": np.ascontiguousarray(w1.T), "w2t": np.ascontiguousarray(w2.T),
        "w1n": w1, "w2n": w2,
        "bt": np.ascontiguousarray(Bm.T), "ct": np.ascontiguousarray(Cm.T),
        "dt": np.ascontiguousarray(Dm.T),
        "b1r": vr(L0_ffn_b1, HC), "b2r": vr(L0_ffn_b2, DC),
        "adr": vr(L0_attn_A_diag, DC), "h0r": vr(L0_attn_h0, DC),
        "gamr": vr(L0_attn_ln_gamma, DC), "betr": vr(L0_attn_ln_beta, DC),
    }
    in_maps = []
    for c in range(NCORES):
        bidx, j = c // 4, c % 4
        base = 4 * bidx
        uc = np.ascontiguousarray(u[bidx, j * T:(j + 1) * T, :].T)
        msk = np.zeros(192, np.float32)
        for i in range(3):
            if j - 1 - i >= 0:
                r = base + j - 1 - i
                for k in range(DC):
                    msk[i * 64 + k * 8 + r] = 1.0
        hm = np.zeros(32, np.float32)
        hm[j * 8:(j + 1) * 8] = 1.0
        m = dict(shared)
        m["u"] = uc
        m["msk"] = rep(msk)
        m["hm"] = rep(hm)
        in_maps.append(m)

    if _NC_CACHE is None:
        _NC_CACHE = build_nc()
    res = run_bass_kernel_spmd(_NC_CACHE, in_maps, core_ids=list(range(NCORES)))

    out = np.empty((2, 8192, 1024), np.float32)
    for c in range(NCORES):
        bidx, j = c // 4, c % 4
        out[bidx, j * T:(j + 1) * T, :] = res.results[c]["out"].T
    return out


if __name__ == "__main__":
    import reference as _r
    ins = _r.setup_inputs(0)
    got = kernel(**{k: np.asarray(v) for k, v in ins.items()})
    exp = np.asarray(_r.reference(**ins))
    num = float(np.mean((got - exp) ** 2))
    den = float(np.mean(exp ** 2))
    print("res_var_ratio:", num / den, " max_abs:", float(np.abs(got - exp).max()))


# version 06
# speedup vs baseline: 1.1268x; 1.1268x over previous
import numpy as np
from contextlib import ExitStack

import concourse.bass as bass
import concourse.mybir as mybir
from concourse.bass_utils import run_bass_kernel_spmd

F32 = mybir.dt.float32
BF16 = mybir.dt.bfloat16
AF = mybir.ActivationFunctionType
ALU = mybir.AluOpType
AX = mybir.AxisListType

D = 1024
DC = 8
H = 4096
HC = 32
T = 2048
NB = 4
TB = 512
NCORES = 8
EPS = 1e-5
CLAMP = 1.0e6


def build_nc():
    nc = bass.Bass("TRN2", target_bir_lowering=False, debug=False)
    es = ExitStack()

    def din(name, shape, dt=F32):
        return nc.dram_tensor(name, shape, dt, kind="ExternalInput").ap()

    u_d = din("u", [D, T])
    w1t_d = din("w1t", [D, H])
    w2t_d = din("w2t", [H, D])
    bt_d = din("bt", [D, D])
    ct_d = din("ct", [D, D])
    dt_d = din("dt", [D, D])
    b1_d = din("b1r", [128, HC])
    b2_d = din("b2r", [128, DC])
    ad_d = din("adr", [128, DC])
    h0_d = din("h0r", [128, DC])
    gam_d = din("gamr", [128, DC])
    bet_d = din("betr", [128, DC])
    msk_d = din("msk", [128, 192])
    hm_d = din("hm", [128, 32])

    out_d = nc.dram_tensor("out", [D, T], F32, kind="ExternalOutput").ap()

    s1_d = nc.dram_tensor("s1sc", [DC, HC, 128, 128], BF16).ap()
    s2_d = nc.dram_tensor("s2sc", [HC, DC, 128, 128], BF16).ap()
    btb_d = nc.dram_tensor("btsc", [DC, DC, 128, 128], BF16).ap()
    ctb_d = nc.dram_tensor("ctsc", [DC, DC, 128, 128], BF16).ap()
    dtb_d = nc.dram_tensor("dtsc", [DC, DC, 128, 128], BF16).ap()
    xs_d = nc.dram_tensor("xssc", [NB, 128, DC * TB], BF16).ap()
    cc_in = nc.dram_tensor("cc_in", [D], F32)
    cc_out = nc.dram_tensor("cc_out", [NCORES, D], F32)
    aw1_d = nc.dram_tensor("aw1sc", [H], F32).ap()
    aw2_d = nc.dram_tensor("aw2sc", [D], F32).ap()

    def sb(name, shape, dt=F32):
        return es.enter_context(nc.sbuf_tensor(name, shape, dt))

    x_t = sb("x", [128, DC * T])
    z1a_t = sb("z1a", [128, DC * TB // 2])
    z2_t = sb("z2t", [128, DC * TB // 2])
    h_t = sb("ht", [128, HC * TB // 2])
    sA_t = sb("sAt", [128, DC * TB // 2])
    sB_t = sb("sBt", [128, DC * TB // 2])
    g0_t = sb("g0t", [128, DC * TB // 2])
    scr_t = sb("scr", [128, 2 * TB])
    bu_t = sb("bu", [128, DC * TB])
    xs_t = sb("xs", [128, TB])
    xb_t = sb("xbt", [128, DC * TB // 2])
    s1b_t = sb("s1b", [128, DC * 128])
    s2b_t = sb("s2b", [128, HC * 128 // 2])
    bcd_t = sb("bcdt", [128, 2 * DC * 128])
    apw_t = sb("apw", [128, TB])
    zz_t = sb("zz", [128, TB])
    rows_t = sb("rows", [1, 5 * TB])
    rbf_t = sb("rbf", [1, 4 * TB], BF16)
    small_t = sb("small", [128, 160])
    aw1_t = sb("aw1", [128, HC])
    b1_t = sb("b1s", [128, HC])
    b2_t = sb("b2s", [128, DC])
    gam_t = sb("gams", [128, DC])
    bet_t = sb("bets", [128, DC])
    msk_t = sb("msks", [128, 192])
    hm_t = sb("hms", [128, 32])
    fg_t = sb("fg", [128, 64])
    onec_t = sb("onec", [128, 1], BF16)
    oner_t = sb("oner", [1, 128], BF16)

    z1b = z1a_t[:, :].bitcast(BF16)
    z2b = z2_t[:, :].bitcast(BF16)
    hb = h_t[:, :].bitcast(BF16)
    sA = sA_t[:, :].bitcast(BF16)
    sB = sB_t[:, :].bitcast(BF16)
    gact = g0_t[:, :].bitcast(BF16)
    xb = xb_t[:, :].bitcast(BF16)
    s1b = s1b_t[:, :].bitcast(BF16)
    s2b = s2b_t[:, :].bitcast(BF16)
    bcd = bcd_t[:, :].bitcast(BF16)
    buxb = bu_t[:, 0:DC * TB // 2].bitcast(BF16)
    xbL = [xb, buxb]

    stage = h_t
    sgnv = bu_t[:, :].bitcast(BF16)
    absb = xb

    SM = {}
    _off = [0]
    for nm, w in [("a", DC), ("A512", DC), ("A1024", DC), ("A1536", DC), ("A2048", DC),
                  ("P2", DC), ("P3", DC), ("hin", DC), ("hcoef", DC), ("tmp8", DC),
                  ("stcol", DC), ("hsc", NB * DC), ("aw2", DC), ("onef", 1)]:
        SM[nm] = small_t[:, _off[0]:_off[0] + w]
        _off[0] += w

    ps_a = es.enter_context(nc.psum_tensor("ps_a", [128, TB], F32))
    ps_b = es.enter_context(nc.psum_tensor("ps_b", [128, TB], F32))
    ps_c = es.enter_context(nc.psum_tensor("ps_c", [128, TB], F32))
    ps_dd = es.enter_context(nc.psum_tensor("ps_d", [128, TB], F32))
    ps_s1 = es.enter_context(nc.psum_tensor("ps_s1", [128, TB], F32))
    ps_s2 = es.enter_context(nc.psum_tensor("ps_s2", [128, TB], F32))
    ps_rb = es.enter_context(nc.psum_tensor("ps_rb", [128, TB], F32))
    ps_qb = es.enter_context(nc.psum_tensor("ps_qb", [128, TB], F32))
    psQ = [ps_a, ps_b, ps_c, ps_dd]
    psALL = [ps_a, ps_b, ps_c, ps_dd, ps_s1, ps_s2, ps_rb, ps_qb]

    sems = {}
    DMA_SEMS = ["yin", "ysg0", "ysg1", "ywsa", "ywsb", "ws1a", "ws1b", "ws2a", "ws2b",
                "wbca", "wbcb", "wbda", "wbdb", "yzsz", "yzsx", "yzlz", "yzlx", "yout",
                "gd", "wal"]
    TOT_SEMS = {"yin", "yout"}
    for nm in ["t", "s", "v", "cc"] + DMA_SEMS:
        sems[nm] = es.enter_context(nc.semaphore(nm))

    cnt = {k: 0 for k in sems}
    emits = {k: [] for k in ["t", "s", "v", "g", "y"]}

    def op(eng, fn, inc=None, waits=()):
        ws = {}
        for tok in waits:
            if tok is None:
                continue
            sn, sv = tok
            if sn in ws and (ws[sn] == -1 or (sv != -1 and ws[sn] >= sv)):
                continue
            ws[sn] = sv
        emits[eng].append((tuple(ws.items()), fn, inc))
        if inc is None:
            return None
        cnt[inc] += 16 if inc in DMA_SEMS else 1
        return (inc, cnt[inc])

    def dma(sem, outap, inap, waits=(), tot=None):
        eng = "g" if sem == "gd" else "y"
        q = nc.gpsimd if sem == "gd" else nc.sync
        tok = op(eng, lambda o=outap, i=inap, q=q: q.dma_start(out=o, in_=i),
                 inc=sem, waits=waits)
        if tot is None:
            tot = sem in TOT_SEMS
        return (sem, -1) if tot else tok

    def vop(fn, waits=()):
        return op("v", fn, inc="v", waits=waits)

    def sop(fn, waits=()):
        return op("s", fn, inc="s", waits=waits)

    def top(fn, waits=()):
        return op("t", fn, inc="t", waits=waits)

    def top_sil(fn, waits=()):
        return op("t", fn, None, waits=waits)

    V, S, TE, G = nc.vector, nc.scalar, nc.tensor, nc.gpsimd

    last = {}

    def LU(key):
        return list(last.get(key, []))

    def setLU(key, toks):
        last[key] = [t for t in toks if t is not None]

    def addLU(key, toks):
        last[key] = LU(key) + [t for t in toks if t is not None]

    _psctr = [0]

    def nxt_ps():
        i = _psctr[0] % 4
        _psctr[0] += 1
        return i

    vop(lambda: V.memset(onec_t[:, :], 1.0))
    vop(lambda: V.memset(oner_t[:, :], 1.0))
    vop(lambda: V.memset(zz_t[:, :], 0.0))
    vop(lambda: V.memset(SM["stcol"], 0.0))
    vop(lambda: V.memset(SM["onef"], 1.0))

    x_load_toks = []
    for k in range(DC):
        x_load_toks.append(dma("yin", x_t[:, k * T:(k + 1) * T],
                               u_d[k * 128:(k + 1) * 128, :]))
    sv_toks = []
    for dst, src in [(b1_t[:, :], b1_d), (b2_t[:, :], b2_d), (gam_t[:, :], gam_d),
                     (bet_t[:, :], bet_d), (msk_t[:, :], msk_d), (hm_t[:, :], hm_d)]:
        sv_toks.append(dma("yin", dst, src))
    tok_ad = dma("yin", SM["tmp8"], ad_d)
    tok_h0 = dma("yin", SM["hcoef"], h0_d)

    t_e = sop(lambda: S.activation(out=SM["a"], in_=SM["tmp8"], func=AF.Exp),
              waits=[tok_ad])
    t_sp = sop(lambda: S.activation(out=SM["a"], in_=SM["a"], func=AF.Ln, bias=1.0),
               waits=[t_e])
    t_a = vop(lambda: V.tensor_scalar_mul(SM["a"], SM["a"], -1.0), waits=[t_sp])
    t_p = vop(lambda: V.tensor_copy(SM["A512"], SM["a"]), waits=[t_a])
    for _ in range(9):
        t_p = vop(lambda: V.tensor_tensor(out=SM["A512"], in0=SM["A512"],
                                          in1=SM["A512"], op=ALU.mult), waits=[t_p])
    t_1024 = vop(lambda: V.tensor_tensor(out=SM["A1024"], in0=SM["A512"], in1=SM["A512"],
                                         op=ALU.mult), waits=[t_p])
    t_1536 = vop(lambda: V.tensor_tensor(out=SM["A1536"], in0=SM["A1024"], in1=SM["A512"],
                                         op=ALU.mult), waits=[t_1024])
    t_2048 = vop(lambda: V.tensor_tensor(out=SM["A2048"], in0=SM["A1024"], in1=SM["A1024"],
                                         op=ALU.mult), waits=[t_1024])
    t_P2 = vop(lambda: V.tensor_tensor(out=SM["P2"], in0=SM["A2048"], in1=SM["A2048"],
                                       op=ALU.mult), waits=[t_2048])
    t_P3 = vop(lambda: V.tensor_tensor(out=SM["P3"], in0=SM["P2"], in1=SM["A2048"],
                                       op=ALU.mult), waits=[t_P2])

    wprep = {"s1": [], "s2": [], "bt": [], "ct": [], "dt": []}

    a1_mm_tok = [None] * 8
    abs_last = []
    for k in range(DC):
        q = k % 2
        st_sl = stage[:, q * 4096:(q + 1) * 4096]
        sg_sl = sgnv[:, q * 4096:(q + 1) * 4096]
        tl = dma("ysg%d" % q, st_sl, w1t_d[k * 128:(k + 1) * 128, :], waits=LU(("stage", q)))
        ts = sop(lambda o=sg_sl, i=st_sl: S.activation(out=o, in_=i, func=AF.Sign),
                 waits=[tl] + LU(("sgn", q)))
        tab = sop(lambda i=st_sl: S.activation(out=absb, in_=i, func=AF.Abs),
                  waits=[tl] + abs_last)
        setLU(("stage", q), [ts, tab])
        td = dma("ywsa" if q == 0 else "ywsb", s1_d[k].rearrange("c p j -> p c j"),
                 sg_sl.rearrange("p (c j) -> p c j", c=HC, j=128), waits=[ts])
        setLU(("sgn", q), [td])
        wprep["s1"].append(td)
        abs_last = []
        for j in range(8):
            tm = (lambda j=j, k=k: TE.matmul(
                psALL[j][0:1, :], onec_t[:, :], absb[:, j * TB:(j + 1) * TB],
                start=(k == 0), stop=(k == DC - 1)))
            if k == DC - 1:
                a1_mm_tok[j] = top(tm, waits=[tab])
                abs_last.append(a1_mm_tok[j])
            else:
                abs_last.append(top(tm, waits=[tab]))

    a1_st_toks = []
    rstage = [rows_t[0:1, 0:TB], rows_t[0:1, TB:2 * TB]]
    rst_lu = [[], []]
    for j in range(8):
        rq = j % 2
        cp = sop(lambda j=j, rq=rq: S.copy(out=rstage[rq], in_=psALL[j][0:1, :]),
                 waits=[a1_mm_tok[j]] + rst_lu[rq])
        td = dma("wal", aw1_d[j * TB:(j + 1) * TB].unsqueeze(0),
                 rstage[rq], waits=[cp])
        rst_lu[rq] = [td]
        a1_st_toks.append(td)
        setLU(("ps_alpha", j), [cp])
    a1_dma_toks = []
    for c in range(HC):
        a1_dma_toks.append(dma("wal", aw1_t[:, c:c + 1],
                               aw1_d[c * 128:(c + 1) * 128].unsqueeze(1),
                               waits=[a1_st_toks[c // 4]]))

    a2_mm_tok = [None] * 2
    abs_last2 = list(abs_last)
    for g in range(8):
        q = g % 2
        st_sl = stage[:, q * 4096:(q + 1) * 4096]
        sg_sl = sgnv[:, q * 4096:(q + 1) * 4096]
        tl = dma("ysg%d" % q, st_sl.rearrange("p (c j) -> p c j", c=4, j=D),
                 w2t_d[g * 512:(g + 1) * 512, :].rearrange("(c p) j -> p c j", c=4, p=128),
                 waits=LU(("stage", q)))
        ts = sop(lambda o=sg_sl, i=st_sl: S.activation(out=o, in_=i, func=AF.Sign),
                 waits=[tl] + LU(("sgn", q)))
        tab = sop(lambda i=st_sl: S.activation(out=absb, in_=i, func=AF.Abs),
                  waits=[tl] + abs_last2)
        setLU(("stage", q), [ts, tab])
        tds = []
        for i in range(4):
            k = g * 4 + i
            td = dma("ywsa" if q == 0 else "ywsb", s2_d[k].rearrange("c p j -> p c j"),
                     sg_sl[:, i * D:(i + 1) * D].rearrange("p (c j) -> p c j", c=DC, j=128),
                     waits=[ts])
            tds.append(td)
            wprep["s2"].append(td)
        setLU(("sgn", q), tds)
        abs_last2 = []
        for i in range(4):
            for j in range(2):
                first = (g == 0 and i == 0)
                lastm = (g == 7 and i == 3)
                w = [tab] + (LU(("ps_alpha", j)) if first else [])
                tm = (lambda i=i, j=j, first=first, lastm=lastm: TE.matmul(
                    psALL[j][0:1, :], onec_t[:, :],
                    absb[:, i * D + j * TB:i * D + (j + 1) * TB],
                    start=first, stop=lastm))
                if lastm:
                    a2_mm_tok[j] = top(tm, waits=w)
                    abs_last2.append(a2_mm_tok[j])
                else:
                    abs_last2.append(top(tm, waits=w))

    a2_st_toks = []
    for j in range(2):
        rq = j % 2
        cp = sop(lambda j=j, rq=rq: S.copy(out=rstage[rq], in_=psALL[j][0:1, :]),
                 waits=[a2_mm_tok[j]] + rst_lu[rq])
        td = dma("wal", aw2_d[j * TB:(j + 1) * TB].unsqueeze(0),
                 rstage[rq], waits=[cp])
        rst_lu[rq] = [td]
        a2_st_toks.append(td)
        setLU(("ps_alpha2", j), [cp])
    a2_dma_toks = []
    for c in range(DC):
        a2_dma_toks.append(dma("wal", SM["aw2"][:, c:c + 1],
                               aw2_d[c * 128:(c + 1) * 128].unsqueeze(1),
                               waits=[a2_st_toks[c // 4]]))
    setLU("rows", rst_lu[0] + rst_lu[1])

    for name, src_d, dst_d in [("bt", bt_d, btb_d), ("ct", ct_d, ctb_d),
                               ("dt", dt_d, dtb_d)]:
        for k in range(DC):
            q = k % 2
            st_sl = stage[:, q * 4096:q * 4096 + D]
            sg_sl = sgnv[:, q * 4096:q * 4096 + D]
            tl = dma("ysg%d" % q, st_sl, src_d[k * 128:(k + 1) * 128, :], waits=LU(("stage", q)))
            tv = vop(lambda o=sg_sl, i=st_sl: V.tensor_copy(o, i),
                     waits=[tl] + LU(("sgn", q)))
            setLU(("stage", q), [tv])
            td = dma("ywsa" if q == 0 else "ywsb", dst_d[k].rearrange("c p j -> p c j"),
                     sg_sl.rearrange("p (c j) -> p c j", c=DC, j=128), waits=[tv])
            setLU(("sgn", q), [td])
            wprep[name].append(td)

    t_aw1 = vop(lambda: V.tensor_scalar_mul(aw1_t[:, :], aw1_t[:, :], 1.0 / D),
                waits=a1_dma_toks)
    t_aw2 = vop(lambda: V.tensor_scalar_mul(SM["aw2"], SM["aw2"], 0.5 / H),
                waits=a2_dma_toks)
    t_b2h = vop(lambda: V.tensor_scalar_mul(b2_t[:, :], b2_t[:, :], 0.5), waits=sv_toks)

    setLU("hb_init", LU(("stage", 0)) + LU(("stage", 1)))
    setLU("bu_init", LU(("sgn", 0)) + LU(("sgn", 1)))
    setLU(("xbuf", 0), abs_last2 + a2_mm_tok)
    for j in range(8):
        if j < 4:
            setLU(("ps", j), LU(("ps_alpha", j)) + LU(("ps_alpha2", j)))
    setLU("ps_s", LU(("ps_alpha", 4)) + LU(("ps_alpha", 5)))
    setLU("ps_rb", LU(("ps_alpha", 6)))
    setLU("ps_qb", LU(("ps_alpha", 7)))

    ROW = {nm: rows_t[0:1, i * TB:(i + 1) * TB] for i, nm in
           enumerate(["m", "v", "t", "r", "q"])}
    RBF = {nm: rbf_t[0:1, i * TB:(i + 1) * TB] for i, nm in
           enumerate(["rhi", "rlo", "qhi", "qlo"])}

    def ln_stats(src_slices, src_is_bf, chunk_waits, pre=None, alt=False):
        if pre is not None:
            tA, tB = pre
            movA = [sA[:, k * TB:(k + 1) * TB] for k in range(DC)]
        else:
            tA, tB = [], []
            for k in range(DC):
                sl = src_slices[k]
                if src_is_bf:
                    tA.append(None)
                else:
                    tA.append(vop(lambda o=sA[:, k * TB:(k + 1) * TB], i=sl:
                                  V.tensor_copy(o, i),
                                  waits=chunk_waits[k] + LU(("sAc", k))))
                tB.append(vop(lambda o=sB[:, k * TB:(k + 1) * TB], i=sl:
                              V.tensor_tensor(out=o, in0=i, in1=i, op=ALU.mult),
                              waits=chunk_waits[k] + LU(("sBc", k))))
            movA = (src_slices if src_is_bf
                    else [sA[:, k * TB:(k + 1) * TB] for k in range(DC)])
        tok_s1 = tok_s2 = None
        for k in range(DC):
            w1 = ([tA[k]] if tA and tA[k] is not None else list(chunk_waits[k]))
            if k == 0:
                w1 = w1 + LU("ps_s")
            f1 = (lambda k=k, s=movA[k]: TE.matmul(ps_s1[0:1, :], onec_t[:, :], s,
                                                   start=(k == 0), stop=(k == DC - 1)))
            if k == DC - 1:
                tok_s1 = top(f1, waits=w1)
            else:
                top_sil(f1, waits=w1)
            w2 = [tB[k]] if tB else list(chunk_waits[k])
            f2 = (lambda k=k, s=sB[:, k * TB:(k + 1) * TB]: TE.matmul(
                ps_s2[0:1, :], onec_t[:, :], s, start=(k == 0), stop=(k == DC - 1)))
            if k == DC - 1:
                tok_s2 = top(f2, waits=w2)
            else:
                top_sil(f2, waits=w2)
        for k in range(DC):
            setLU(("sAc", k), [tok_s1])
            setLU(("sBc", k), [tok_s2])
        wr = LU("rows")
        t1 = vop(lambda: V.tensor_scalar_mul(ROW["m"], ps_s1[0:1, :], 1.0 / D),
                 waits=[tok_s1] + wr)
        t2 = vop(lambda: V.tensor_scalar(out=ROW["v"], in0=ps_s2[0:1, :], scalar1=1.0 / D,
                                         scalar2=EPS, op0=ALU.mult, op1=ALU.add),
                 waits=[tok_s2] + wr)
        vop(lambda: V.tensor_tensor(out=ROW["t"], in0=ROW["m"], in1=ROW["m"],
                                    op=ALU.mult), waits=[t1])
        t4 = vop(lambda: V.tensor_tensor(out=ROW["v"], in0=ROW["v"], in1=ROW["t"],
                                         op=ALU.subtract), waits=[t2])
        setLU("ps_s", [t1, t2])
        t5a = vop(lambda: V.reciprocal(out=ROW["t"], in_=ROW["v"]), waits=[t4])
        t5 = sop(lambda: S.activation(out=ROW["r"], in_=ROW["t"], func=AF.Sqrt),
                 waits=[t5a])
        t6 = vop(lambda: V.tensor_tensor(out=ROW["t"], in0=ROW["r"], in1=ROW["r"],
                                         op=ALU.mult), waits=[t5])
        vop(lambda: V.tensor_tensor(out=ROW["t"], in0=ROW["t"], in1=ROW["v"],
                                    op=ALU.mult), waits=[t6])
        vop(lambda: V.tensor_scalar(out=ROW["t"], in0=ROW["t"], scalar1=-0.5,
                                    scalar2=1.5, op0=ALU.mult, op1=ALU.add))
        t9 = vop(lambda: V.tensor_tensor(out=ROW["r"], in0=ROW["r"], in1=ROW["t"],
                                         op=ALU.mult))
        tq = vop(lambda: V.scalar_tensor_tensor(out=ROW["q"], in0=ROW["m"], scalar=-1.0,
                                                in1=ROW["r"], op0=ALU.mult, op1=ALU.mult),
                 waits=[t9])
        wb = LU("rbf")
        c1 = vop(lambda: V.tensor_copy(RBF["rhi"], ROW["r"]), waits=[t9] + wb)
        c2 = vop(lambda: V.tensor_tensor(out=RBF["rlo"], in0=ROW["r"], in1=RBF["rhi"],
                                         op=ALU.subtract), waits=[c1])
        c3 = vop(lambda: V.tensor_copy(RBF["qhi"], ROW["q"]), waits=[tq] + wb)
        c4 = vop(lambda: V.tensor_tensor(out=RBF["qlo"], in0=ROW["q"], in1=RBF["qhi"],
                                         op=ALU.subtract), waits=[c3])
        setLU("rows", [c2, c4])
        if alt:
            pbr, pbq, lur, luq = ps_s1, ps_s2, "ps_s", "ps_s"
        else:
            pbr, pbq, lur, luq = ps_rb, ps_qb, "ps_rb", "ps_qb"
        top_sil(lambda: TE.matmul(pbr[:, :], oner_t[:, :], RBF["rhi"], start=True,
                                  stop=False), waits=[c2, t1, t2] + LU(lur))
        tok_rb = top(lambda: TE.matmul(pbr[:, :], oner_t[:, :], RBF["rlo"], start=False,
                                       stop=True))
        top_sil(lambda: TE.matmul(pbq[:, :], oner_t[:, :], RBF["qhi"], start=True,
                                  stop=False), waits=[c4] + LU(luq))
        tok_qb = top(lambda: TE.matmul(pbq[:, :], oner_t[:, :], RBF["qlo"], start=False,
                                       stop=True))
        setLU("rbf", [tok_rb, tok_qb])
        if alt:
            setLU("ps_s", [tok_rb, tok_qb])
        return tok_rb, tok_qb, pbr, pbq, lur, luq

    def ln_to_z(src_slices, zdst, zkey, lnres, chunk_waits):
        tok_rb, tok_qb, pbr, pbq, lur, luq = lnres
        z_toks = []
        for k in range(DC):
            t1 = vop(lambda k=k, s=src_slices[k]: V.tensor_tensor(
                out=scr_t[:, (k % 2) * TB:(k % 2 + 1) * TB], in0=s, in1=pbr[:, :],
                op=ALU.mult),
                waits=[tok_rb] + chunk_waits[k] + LU(("scr", k % 2)))
            t2 = vop(lambda k=k: V.tensor_tensor(
                out=zdst[:, k * TB:(k + 1) * TB],
                in0=scr_t[:, (k % 2) * TB:(k % 2 + 1) * TB],
                in1=pbq[:, :], op=ALU.add), waits=[tok_qb] + LU((zkey, k)))
            setLU(("scr", k % 2), [t2])
            setLU((zkey, k), [t2])
            z_toks.append(t2)
        addLU(lur, z_toks)
        addLU(luq, z_toks)
        return z_toks

    def ffn_block(b, z_toks, zsrc, zkey, emit_ln_casts):
        gelu_toks = []
        z_read = None
        for c in range(HC):
            q4 = c % 2
            buf = s1b[:, q4 * DC * 128:(q4 + 1) * DC * 128]
            tl = dma("ws1a" if q4 == 0 else "ws1b",
                     buf.rearrange("p (k j) -> p k j", k=DC, j=128),
                     s1_d[:, c].rearrange("k p j -> p k j"),
                     waits=LU(("s1b", q4)) + (wprep["s1"] if c < 2 else []))
            psi = nxt_ps()
            mm_w = [tl] + LU(("ps", psi)) + (z_toks if c == 0 else [])
            tok_mm = None
            for k in range(DC):
                f = (lambda k=k, psi=psi, buf=buf: TE.matmul(
                    psQ[psi][:, :], buf[:, k * 128:(k + 1) * 128],
                    zsrc[:, k * TB:(k + 1) * TB], start=(k == 0), stop=(k == DC - 1)))
                if k == DC - 1:
                    tok_mm = top(f)
                else:
                    top_sil(f, waits=mm_w if k == 0 else ())
            setLU(("s1b", q4), [tok_mm])
            z_read = tok_mm
            tg = sop(lambda c=c, psi=psi: S.activation(
                out=hb[:, c * TB:(c + 1) * TB], in_=psQ[psi][:, :], func=AF.Gelu,
                scale=aw1_t[:, c:c + 1], bias=b1_t[:, c:c + 1]),
                waits=[tok_mm, t_aw1] + sv_toks + (LU("hb_init") if c == 0 else []))
            gelu_toks.append(tg)
            setLU(("ps", psi), [tg])
        setLU("hb_init", [])
        for k in range(DC):
            addLU((zkey, k), [z_read])
        x_toks, cA, cB = [], [], []
        for c in range(DC):
            psi = nxt_ps()
            tok_mm = None
            for half in range(2):
                hq = (2 * c + half) % 2
                buf = s2b[:, hq * 16 * 128:(hq + 1) * 16 * 128]
                tl = dma("ws2a" if hq == 0 else "ws2b",
                         buf.rearrange("p (k j) -> p k j", k=16, j=128),
                         s2_d[half * 16:(half + 1) * 16, c].rearrange("k p j -> p k j"),
                         waits=LU(("s2b", hq)) + (wprep["s2"] if c < 1 else []))
                mm_w = [tl] + ((LU(("ps", psi)) + gelu_toks) if half == 0 else [])
                for k16 in range(16):
                    k = half * 16 + k16
                    f = (lambda k=k, k16=k16, psi=psi, buf=buf: TE.matmul(
                        psQ[psi][:, :], buf[:, k16 * 128:(k16 + 1) * 128],
                        hb[:, k * TB:(k + 1) * TB], start=(k == 0), stop=(k == HC - 1)))
                    if k16 == 15:
                        tok_mm = top(f)
                    else:
                        top_sil(f, waits=mm_w if k16 == 0 else ())
                setLU(("s2b", hq), [tok_mm])
            te = sop(lambda c=c, psi=psi: S.activation(
                out=scr_t[:, (c % 2) * TB:(c % 2 + 1) * TB], in_=psQ[psi][:, :],
                func=AF.Identity,
                scale=SM["aw2"][:, c:c + 1], bias=b2_t[:, c:c + 1]),
                waits=[tok_mm] + LU(("scr", c % 2)) + [t_aw2, t_b2h])
            xsl = x_t[:, c * T + b * TB:c * T + (b + 1) * TB]
            tx = vop(lambda c=c, xsl=xsl: V.tensor_tensor(
                out=xsl, in0=xsl,
                in1=scr_t[:, (c % 2) * TB:(c % 2 + 1) * TB], op=ALU.add), waits=[te])
            setLU(("scr", c % 2), [tx])
            setLU(("ps", psi), [te])
            x_toks.append(tx)
            if emit_ln_casts:
                cA.append(vop(lambda c=c, xsl=xsl: V.tensor_copy(
                    sA[:, c * TB:(c + 1) * TB], xsl), waits=[tx] + LU(("sAc", c))))
                cB.append(vop(lambda c=c, xsl=xsl: V.tensor_tensor(
                    out=sB[:, c * TB:(c + 1) * TB], in0=xsl, in1=xsl, op=ALU.mult),
                    waits=[tx] + LU(("sBc", c))))
        return x_toks, cA, cB

    def b_stage(b, z2_toks):
        bu_toks = []
        for c in range(DC):
            q = c % 2
            buf = bcd[:, q * DC * 128:(q + 1) * DC * 128]
            tl = dma("wbca" if q == 0 else "wbcb",
                     buf.rearrange("p (k j) -> p k j", k=DC, j=128),
                     btb_d[:, c].rearrange("k p j -> p k j"),
                     waits=LU(("bcd", q)) + (wprep["bt"] if (b == 0 and c < 2) else []))
            psi = nxt_ps()
            mm_w = [tl] + LU(("ps", psi)) + (z2_toks if c == 0 else [])
            tok_mm = None
            for k in range(DC):
                f = (lambda k=k, psi=psi, buf=buf: TE.matmul(
                    psQ[psi][:, :], buf[:, k * 128:(k + 1) * 128],
                    z2b[:, k * TB:(k + 1) * TB], start=(k == 0), stop=(k == DC - 1)))
                if k == DC - 1:
                    tok_mm = top(f)
                else:
                    top_sil(f, waits=mm_w if k == 0 else ())
            setLU(("bcd", q), [tok_mm])
            addLU("z2use", [tok_mm])
            tb_ = sop(lambda c=c, psi=psi: S.copy(out=bu_t[:, c * TB:(c + 1) * TB],
                                                  in_=psQ[psi][:, :]),
                      waits=[tok_mm] + LU(("bu", c)) + (LU("bu_init") if (b == 0 and c == 0) else []))
            setLU(("ps", psi), [tb_])
            bu_toks.append(tb_)
        return bu_toks

    stcol_tok = {k: None for k in range(DC)}
    xs_store_toks = []

    def scan_stage(b, bu_toks):
        tcb_toks = []
        for k in range(DC):
            tsc = vop(lambda k=k: V.tensor_tensor_scan(
                out=xs_t[:, :],
                data0=SM["a"][:, k:k + 1].broadcast_to((128, TB)),
                data1=bu_t[:, k * TB:(k + 1) * TB],
                initial=SM["stcol"][:, k:k + 1],
                op0=ALU.mult, op1=ALU.add),
                waits=[bu_toks[k], t_a, stcol_tok[k]] + LU("xs"))
            setLU(("bu", k), [tsc])
            tst = vop(lambda k=k: V.tensor_copy(
                SM["stcol"][:, k:k + 1], xs_t[:, TB - 1:TB]),
                waits=[tsc])
            stcol_tok[k] = tst
            tcb = vop(lambda k=k: V.tensor_copy(xb[:, k * TB:(k + 1) * TB],
                                                xs_t[:, :]),
                      waits=[tsc] + (LU(("xbuf", 0)) if k == 0 else []))
            setLU("xs", [tcb])
            tcb_toks.append(tcb)
        txs = dma("yzsx", xs_d[b], xb[:, :], waits=tcb_toks)
        setLU(("xbuf", 0), [txs])
        xs_store_toks.append(txs)
        return tcb_toks

    def xsl_blk(b):
        return [x_t[:, k * T + b * TB:k * T + (b + 1) * TB] for k in range(DC)]

    ch_w_in = [[x_load_toks[k]] for k in range(DC)]
    ln1 = ln_stats(xsl_blk(0), False, ch_w_in)
    z1_toks = ln_to_z(xsl_blk(0), z1b, "z1k", ln1, ch_w_in)

    for b in range(NB):
        x_toks, cA, cB = ffn_block(b, z1_toks, z1b, "z1k", True)
        ln2 = ln_stats(None, False, [[t] for t in x_toks], pre=(cA, cB))
        ch_w = [[x_toks[k]] for k in range(DC)]
        z2_toks = ln_to_z(xsl_blk(b), z2b, "z2k", ln2, ch_w)
        addLU("z2use", z2_toks)
        if b + 1 < NB:
            chn = [[x_load_toks[k]] for k in range(DC)]
            ln1 = ln_stats(xsl_blk(b + 1), False, chn)
            z1_toks = ln_to_z(xsl_blk(b + 1), z1b, "z1k", ln1, chn)
        bu_toks = b_stage(b, z2_toks)
        for k in range(DC):
            setLU(("z2k", k), LU("z2use"))
        setLU("z2use", [])
        scan_stage(b, bu_toks)

    z2w = {}
    no_w = [[] for _ in range(DC)]
    ln2p = ln_stats(xsl_blk(0), False, no_w, alt=True)
    z2w[0] = ln_to_z(xsl_blk(0), z2b, "z2k", ln2p, no_w)

    cc_dmas = []
    for k in range(DC):
        cc_dmas.append(dma("gd", cc_in.ap()[k * 128:(k + 1) * 128].unsqueeze(1),
                           SM["stcol"][:, k:k + 1],
                           waits=[stcol_tok[kk] for kk in range(DC)], tot=False))
    tok_cc = op("g", lambda: G.collective_compute(
        "AllGather", ALU.bypass, replica_groups=[list(range(NCORES))],
        ins=[cc_in.ap().opt()], outs=[cc_out.ap().opt()]), inc="cc", waits=cc_dmas)
    fg_toks = []
    for k in range(DC):
        for r in range(NCORES):
            fg_toks.append(dma("gd", fg_t[:, k * 8 + r:k * 8 + r + 1],
                               cc_out.ap()[r, k * 128:(k + 1) * 128].unsqueeze(1),
                               waits=[tok_cc], tot=True))

    t_h0mv = vop(lambda: V.tensor_copy(SM["hsc"][:, 0:DC], SM["hcoef"]), waits=[tok_h0])
    h0_ap = SM["hsc"][:, 0:DC]
    fsel_aps = [SM["tmp8"], SM["hcoef"], SM["A1024"]]
    fsel_scr = bu_t[:, 4 * TB:4 * TB + 192]
    scan3_reads = [stcol_tok[k] for k in range(DC)] + LU("xs")
    fsel_toks = []
    for i in range(3):
        tm = vop(lambda i=i: V.tensor_tensor(
            out=fsel_scr[:, i * 64:(i + 1) * 64], in0=fg_t[:, :],
            in1=msk_t[:, i * 64:(i + 1) * 64], op=ALU.mult),
            waits=fg_toks + sv_toks + scan3_reads)
        tr = vop(lambda i=i, d=fsel_aps[i]: V.tensor_reduce(
            out=d, in_=fsel_scr[:, i * 64:(i + 1) * 64]
            .rearrange("p (k r) -> p k r", k=8, r=8),
            axis=AX.X, op=ALU.add), waits=[tm])
        fsel_toks.append(tr)
    vop(lambda: V.tensor_copy(SM["hin"], fsel_aps[0]), waits=fsel_toks)
    vop(lambda: V.scalar_tensor_tensor(out=SM["A1536"], in0=fsel_aps[1], scalar=1.0,
                                       in1=SM["A2048"], op0=ALU.mult, op1=ALU.mult))
    vop(lambda: V.tensor_tensor(out=SM["hin"], in0=SM["hin"], in1=SM["A1536"],
                                op=ALU.add))
    vop(lambda: V.tensor_tensor(out=SM["A1536"], in0=fsel_aps[2], in1=SM["P2"],
                                op=ALU.mult))
    vop(lambda: V.tensor_tensor(out=SM["hin"], in0=SM["hin"], in1=SM["A1536"],
                                op=ALU.add))
    vop(lambda: V.tensor_copy(SM["A1536"], hm_t[:, 0:8]))
    vop(lambda: V.tensor_tensor(out=SM["P2"], in0=SM["A2048"], in1=hm_t[:, 8:16],
                                op=ALU.mult))
    vop(lambda: V.tensor_tensor(out=SM["A1536"], in0=SM["A1536"], in1=SM["P2"],
                                op=ALU.add))
    vop(lambda: V.tensor_tensor(out=SM["P2"], in0=SM["A2048"], in1=SM["A2048"],
                                op=ALU.mult))
    vop(lambda: V.tensor_tensor(out=SM["tmp8"], in0=SM["P2"], in1=hm_t[:, 16:24],
                                op=ALU.mult))
    vop(lambda: V.tensor_tensor(out=SM["A1536"], in0=SM["A1536"], in1=SM["tmp8"],
                                op=ALU.add))
    vop(lambda: V.tensor_tensor(out=SM["tmp8"], in0=SM["P3"], in1=hm_t[:, 24:32],
                                op=ALU.mult), waits=[t_P3])
    vop(lambda: V.tensor_tensor(out=SM["A1536"], in0=SM["A1536"], in1=SM["tmp8"],
                                op=ALU.add))
    vop(lambda: V.tensor_tensor(out=SM["tmp8"], in0=SM["A1536"], in1=h0_ap,
                                op=ALU.mult), waits=[t_h0mv])
    th_in = vop(lambda: V.tensor_tensor(out=SM["hin"], in0=SM["hin"], in1=SM["tmp8"],
                                        op=ALU.add))
    ts0 = vop(lambda: V.tensor_copy(SM["hsc"][:, 0:8], SM["hin"]), waits=[th_in])
    ts1 = vop(lambda: V.tensor_tensor(out=SM["hsc"][:, 8:16], in0=SM["hin"],
                                      in1=SM["A512"], op=ALU.mult))
    ts2 = vop(lambda: V.tensor_tensor(out=SM["hsc"][:, 16:24], in0=SM["hsc"][:, 8:16],
                                      in1=SM["A512"], op=ALU.mult))
    ts3 = vop(lambda: V.tensor_tensor(out=SM["hsc"][:, 24:32], in0=SM["hsc"][:, 16:24],
                                      in1=SM["A512"], op=ALU.mult))
    hsc_toks = [ts0, ts1, ts2, ts3]

    setLU(("xbuf", 1), [stcol_tok[k] for k in range(4)] + fsel_toks + LU("xs"))

    tlx = [None] * NB
    corr_toks = [None] * NB

    def emit_loads(b):
        tlx[b] = dma("yzlx", xbL[b % 2][:, :], xs_d[b],
                     waits=[xs_store_toks[b]] + LU(("xbuf", b % 2)))
        setLU(("xbuf", b % 2), [tlx[b]])

    def emit_corr(b):
        toks = []
        for k in range(DC):
            ta = vop(lambda k=k: V.tensor_tensor_scan(
                out=apw_t[:, :], data0=SM["a"][:, k:k + 1].broadcast_to((128, TB)),
                data1=zz_t[:, :], initial=SM["onef"][:, 0:1],
                op0=ALU.mult, op1=ALU.add), waits=[t_a] + hsc_toks + LU("apw"))
            tc = vop(lambda k=k, b=b: V.scalar_tensor_tensor(
                out=xbL[b % 2][:, k * TB:(k + 1) * TB], in0=apw_t[:, :],
                scalar=SM["hsc"][:, b * 8 + k:b * 8 + k + 1],
                in1=xbL[b % 2][:, k * TB:(k + 1) * TB], op0=ALU.mult, op1=ALU.add),
                waits=[ta, tlx[b]])
            setLU("apw", [tc])
            addLU(("xbuf", b % 2), [tc])
            toks.append(tc)
        corr_toks[b] = toks

    cd_mm_toks = [[None] * DC for _ in range(NB)]
    cd_psi = [[None] * DC for _ in range(NB)]

    def cd_mms(b, crange):
        tok_mm = None
        for c in crange:
            q = c % 2
            bufC = bcd[:, q * DC * 128:(q + 1) * DC * 128]
            bufD = bcd[:, (2 + q) * DC * 128:(3 + q) * DC * 128]
            tlc = dma("wbca" if q == 0 else "wbcb",
                      bufC.rearrange("p (k j) -> p k j", k=DC, j=128),
                      ctb_d[:, c].rearrange("k p j -> p k j"),
                      waits=LU(("bcd", q)) + (wprep["ct"] if (b == 0 and c < 2) else []))
            tld = dma("wbda" if q == 0 else "wbdb",
                      bufD.rearrange("p (k j) -> p k j", k=DC, j=128),
                      dtb_d[:, c].rearrange("k p j -> p k j"),
                      waits=LU(("bcd", 2 + q)) + (wprep["dt"] if (b == 0 and c < 2) else []))
            psi = nxt_ps()
            mm_w = [tlc, tld] + LU(("ps", psi)) + \
                ((corr_toks[b] + z2w[b]) if c == crange[0] else [])
            for k in range(DC):
                f = (lambda k=k, psi=psi, bufC=bufC, b=b: TE.matmul(
                    psQ[psi][:, :], bufC[:, k * 128:(k + 1) * 128],
                    xbL[b % 2][:, k * TB:(k + 1) * TB], start=(k == 0), stop=False))
                top_sil(f, waits=mm_w if k == 0 else ())
            tok_mm = None
            for k in range(DC):
                f = (lambda k=k, psi=psi, bufD=bufD: TE.matmul(
                    psQ[psi][:, :], bufD[:, k * 128:(k + 1) * 128],
                    z2b[:, k * TB:(k + 1) * TB], start=False, stop=(k == DC - 1)))
                if k == DC - 1:
                    tok_mm = top(f)
                else:
                    top_sil(f)
            setLU(("bcd", q), [tok_mm])
            setLU(("bcd", 2 + q), [tok_mm])
            addLU(("xbuf", b % 2), [tok_mm])
            cd_mm_toks[b][c] = tok_mm
            cd_psi[b][c] = psi
        for k in range(DC):
            addLU(("z2k", k), [tok_mm])

    def cd_evac(b, crange, prev_t1=None):
        for c in crange:
            psi = cd_psi[b][c]
            w = [cd_mm_toks[b][c]] + LU(("gel", c))
            if prev_t1 is not None:
                w = w + [prev_t1[c]]
            tga = sop(lambda c=c, psi=psi: S.activation(
                out=gact[:, c * TB:(c + 1) * TB], in_=psQ[psi][:, :],
                func=AF.Gelu), waits=w)
            setLU(("ps", psi), [tga])
            setLU(("gel", c), [tga])
            gel_toks[b][c] = tga

    gel_toks = [[None] * DC for _ in range(NB)]

    emit_loads(0)
    emit_loads(1)
    emit_corr(0)
    cd_mms(0, list(range(0, 8)))
    cd_evac(0, list(range(0, 8)))
    ln2p = ln_stats(xsl_blk(1), False, no_w, alt=True)
    z2w[1] = ln_to_z(xsl_blk(1), z2b, "z2k", ln2p, no_w)
    emit_corr(1)

    fx = {k: [] for k in range(DC)}
    for b in range(NB):
        gsl = [gact[:, k * TB:(k + 1) * TB] for k in range(DC)]
        rb3, qb3, pbr3, pbq3, lur3, luq3 = ln_stats(
            gsl, True, [[gel_toks[b][k]] for k in range(DC)])
        if b + 2 < NB:
            emit_loads(b + 2)
        if b + 1 < NB:
            cd_mms(b + 1, list(range(0, 4)))
        res_toks = []
        rb_reads, qb_reads = [], []
        for k in range(DC):
            t1 = vop(lambda k=k, s=gsl[k]: V.tensor_tensor(
                out=scr_t[:, (k % 2) * TB:(k % 2 + 1) * TB], in0=s,
                in1=pbr3[:, :], op=ALU.mult),
                waits=[rb3, gel_toks[b][k]] + LU(("scr", k % 2)))
            t2 = vop(lambda k=k: V.tensor_tensor(
                out=scr_t[:, (k % 2) * TB:(k % 2 + 1) * TB],
                in0=scr_t[:, (k % 2) * TB:(k % 2 + 1) * TB],
                in1=pbq3[:, :], op=ALU.add), waits=[qb3])
            rb_reads.append(t1)
            qb_reads.append(t2)
            setLU(("gel", k), [t1, qb3])
            t3 = sop(lambda k=k: S.activation(
                out=scr_t[:, (k % 2) * TB:(k % 2 + 1) * TB],
                in_=scr_t[:, (k % 2) * TB:(k % 2 + 1) * TB],
                func=AF.Identity, scale=gam_t[:, k:k + 1], bias=bet_t[:, k:k + 1]),
                waits=[t2] + sv_toks)
            t4 = vop(lambda k=k: V.tensor_scalar(
                out=scr_t[:, (k % 2) * TB:(k % 2 + 1) * TB],
                in0=scr_t[:, (k % 2) * TB:(k % 2 + 1) * TB],
                scalar1=CLAMP, scalar2=-CLAMP, op0=ALU.min, op1=ALU.max), waits=[t3])
            xsl = x_t[:, k * T + b * TB:k * T + (b + 1) * TB]
            t5 = vop(lambda k=k, xsl=xsl: V.tensor_tensor(
                out=xsl, in0=xsl,
                in1=scr_t[:, (k % 2) * TB:(k % 2 + 1) * TB], op=ALU.add), waits=[t4])
            setLU(("scr", k % 2), [t5])
            res_toks.append(t5)
        setLU("ps_rb", rb_reads)
        setLU("ps_qb", qb_reads)
        if b + 1 < NB:
            cd_evac(b + 1, list(range(0, 4)), prev_t1=rb_reads)
        ln4 = ln_stats(xsl_blk(b), False, [[t] for t in res_toks])
        if b + 1 < NB:
            cd_mms(b + 1, list(range(4, 8)))
            cd_evac(b + 1, list(range(4, 8)), prev_t1=rb_reads)
        z4_toks = ln_to_z(xsl_blk(b), z1b, "z1k", ln4,
                          [[t] for t in res_toks])
        if b + 2 < NB:
            emit_corr(b + 2)
        x3_toks, _, _ = ffn_block(b, z4_toks, z1b, "z1k", False)
        for k in range(DC):
            fx[k].append(x3_toks[k])
        if b + 2 < NB:
            ln2p = ln_stats(xsl_blk(b + 2), False, no_w, alt=True)
            z2w[b + 2] = ln_to_z(xsl_blk(b + 2), z2b, "z2k", ln2p, no_w)

    st_toks = []
    for k in range(DC):
        st_toks.append(dma("yout", out_d[k * 128:(k + 1) * 128, :],
                           x_t[:, k * T:(k + 1) * T], waits=fx[k]))
    op("g", lambda: None, None, waits=st_toks)

    hi_water = {e: {} for e in emits}

    def replay(e, own):
        hw = hi_water[e]
        eng = {"y": nc.sync, "t": nc.tensor, "s": nc.scalar, "v": nc.vector,
               "g": nc.gpsimd}[e]
        kown = 0
        serialize = own in ("s", "v")
        for ws, fn, inc in emits[e]:
            if serialize and kown > 0 and hw.get(own, 0) < kown:
                eng.wait_ge(sems[own], kown)
                hw[own] = kown
            for sn, sv in ws:
                if sn == own and serialize:
                    continue
                if sn == own:
                    continue
                if sv == -1:
                    sv = cnt[sn]
                if hw.get(sn, 0) >= sv:
                    continue
                eng.wait_ge(sems[sn], sv)
                hw[sn] = sv
            ins = fn()
            if inc is not None:
                assert ins is not None, "op expected to return instruction"
                ins.then_inc(sems[inc], 16 if inc in DMA_SEMS else 1)
                if inc == own:
                    kown += 1

    with nc.Block() as block:
        @block.sync
        def _(sync):
            replay("y", None)

        @block.tensor
        def _(te):
            replay("t", "t")

        @block.scalar
        def _(sc):
            replay("s", "s")

        @block.vector
        def _(ve):
            replay("v", "v")

        @block.gpsimd
        def _(gp):
            replay("g", "g")

    es.close()
    return nc


_NC_CACHE = None


def kernel(L0_attn_u, L0_attn_A_diag, L0_attn_Bmat, L0_attn_Cmat, L0_attn_Dmat,
           L0_attn_h0, L0_attn_ln_gamma, L0_attn_ln_beta,
           L0_ffn_w1, L0_ffn_b1, L0_ffn_w2, L0_ffn_b2):
    global _NC_CACHE
    u = np.asarray(L0_attn_u, np.float32)
    w1 = np.ascontiguousarray(np.asarray(L0_ffn_w1, np.float32))
    w2 = np.ascontiguousarray(np.asarray(L0_ffn_w2, np.float32))
    Bm = np.asarray(L0_attn_Bmat, np.float32)
    Cm = np.asarray(L0_attn_Cmat, np.float32)
    Dm = np.asarray(L0_attn_Dmat, np.float32)

    def vr(v, kk):
        return np.ascontiguousarray(np.asarray(v, np.float32).reshape(kk, 128).T)

    def rep(vrow):
        return np.ascontiguousarray(
            np.broadcast_to(vrow[None, :], (128, vrow.shape[0])).astype(np.float32))

    shared = {
        "w1t": np.ascontiguousarray(w1.T), "w2t": np.ascontiguousarray(w2.T),
        "bt": np.ascontiguousarray(Bm.T), "ct": np.ascontiguousarray(Cm.T),
        "dt": np.ascontiguousarray(Dm.T),
        "b1r": vr(L0_ffn_b1, HC), "b2r": vr(L0_ffn_b2, DC),
        "adr": vr(L0_attn_A_diag, DC), "h0r": vr(L0_attn_h0, DC),
        "gamr": vr(L0_attn_ln_gamma, DC), "betr": vr(L0_attn_ln_beta, DC),
    }
    in_maps = []
    for c in range(NCORES):
        bidx, j = c // 4, c % 4
        base = 4 * bidx
        uc = np.ascontiguousarray(u[bidx, j * T:(j + 1) * T, :].T)
        msk = np.zeros(192, np.float32)
        for i in range(3):
            if j - 1 - i >= 0:
                r = base + j - 1 - i
                for k in range(DC):
                    msk[i * 64 + k * 8 + r] = 1.0
        hm = np.zeros(32, np.float32)
        hm[j * 8:(j + 1) * 8] = 1.0
        m = dict(shared)
        m["u"] = uc
        m["msk"] = rep(msk)
        m["hm"] = rep(hm)
        in_maps.append(m)

    if _NC_CACHE is None:
        _NC_CACHE = build_nc()
    res = run_bass_kernel_spmd(_NC_CACHE, in_maps, core_ids=list(range(NCORES)))

    out = np.empty((2, 8192, 1024), np.float32)
    for c in range(NCORES):
        bidx, j = c // 4, c % 4
        out[bidx, j * T:(j + 1) * T, :] = res.results[c]["out"].T
    return out


if __name__ == "__main__":
    import reference as _r
    ins = _r.setup_inputs(0)
    got = kernel(**{k: np.asarray(v) for k, v in ins.items()})
    exp = np.asarray(_r.reference(**ins))
    num = float(np.mean((got - exp) ** 2))
    den = float(np.mean(exp ** 2))
    print("res_var_ratio:", num / den, " max_abs:", float(np.abs(got - exp).max()))


# version 07 (final)
# speedup vs baseline: 1.1314x; 1.0041x over previous
import numpy as np
from contextlib import ExitStack

import concourse.bass as bass
import concourse.mybir as mybir
from concourse.bass_utils import run_bass_kernel_spmd

F32 = mybir.dt.float32
BF16 = mybir.dt.bfloat16
AF = mybir.ActivationFunctionType
ALU = mybir.AluOpType
AX = mybir.AxisListType

D = 1024
DC = 8
H = 4096
HC = 32
T = 2048
NB = 4
TB = 512
NCORES = 8
EPS = 1e-5
CLAMP = 1.0e6


def build_nc():
    nc = bass.Bass("TRN2", target_bir_lowering=False, debug=False)
    es = ExitStack()

    def din(name, shape, dt=F32):
        return nc.dram_tensor(name, shape, dt, kind="ExternalInput").ap()

    u_d = din("u", [D, T])
    w1t_d = din("w1t", [D, H])
    w2t_d = din("w2t", [H, D])
    bt_d = din("bt", [D, D])
    ct_d = din("ct", [D, D])
    dt_d = din("dt", [D, D])
    b1_d = din("b1r", [128, HC])
    b2_d = din("b2r", [128, DC])
    ad_d = din("adr", [128, DC])
    h0_d = din("h0r", [128, DC])
    gam_d = din("gamr", [128, DC])
    bet_d = din("betr", [128, DC])
    msk_d = din("msk", [128, 192])
    hm_d = din("hm", [128, 32])

    out_d = nc.dram_tensor("out", [D, T], F32, kind="ExternalOutput").ap()

    s1_d = nc.dram_tensor("s1sc", [DC, HC, 128, 128], BF16).ap()
    s2_d = nc.dram_tensor("s2sc", [HC, DC, 128, 128], BF16).ap()
    btb_d = nc.dram_tensor("btsc", [DC, DC, 128, 128], BF16).ap()
    ctb_d = nc.dram_tensor("ctsc", [DC, DC, 128, 128], BF16).ap()
    dtb_d = nc.dram_tensor("dtsc", [DC, DC, 128, 128], BF16).ap()
    xs_d = nc.dram_tensor("xssc", [NB, 128, DC * TB], BF16).ap()
    cc_in = nc.dram_tensor("cc_in", [D], F32)
    cc_out = nc.dram_tensor("cc_out", [NCORES, D], F32)
    aw1_d = nc.dram_tensor("aw1sc", [H], F32).ap()
    aw2_d = nc.dram_tensor("aw2sc", [D], F32).ap()

    def sb(name, shape, dt=F32):
        return es.enter_context(nc.sbuf_tensor(name, shape, dt))

    x_t = sb("x", [128, DC * T])
    z1a_t = sb("z1a", [128, DC * TB // 2])
    z2_t = sb("z2t", [128, DC * TB // 2])
    h_t = sb("ht", [128, HC * TB // 2])
    sA_t = sb("sAt", [128, DC * TB // 2])
    sB_t = sb("sBt", [128, DC * TB // 2])
    g0_t = sb("g0t", [128, DC * TB // 2])
    scr_t = sb("scr", [128, 2 * TB])
    bu_t = sb("bu", [128, DC * TB])
    xs_t = sb("xs", [128, TB])
    xb_t = sb("xbt", [128, DC * TB // 2])
    s1b_t = sb("s1b", [128, DC * 128])
    s2b_t = sb("s2b", [128, HC * 128 // 2])
    bcd_t = sb("bcdt", [128, 2 * DC * 128])
    apw_t = sb("apw", [128, TB])
    zz_t = sb("zz", [128, TB])
    rows_t = sb("rows", [1, 5 * TB])
    rbf_t = sb("rbf", [1, 4 * TB], BF16)
    small_t = sb("small", [128, 160])
    aw1_t = sb("aw1", [128, HC])
    b1_t = sb("b1s", [128, HC])
    b2_t = sb("b2s", [128, DC])
    gam_t = sb("gams", [128, DC])
    bet_t = sb("bets", [128, DC])
    msk_t = sb("msks", [128, 192])
    hm_t = sb("hms", [128, 32])
    fg_t = sb("fg", [128, 64])
    onec_t = sb("onec", [128, 1], BF16)
    oner_t = sb("oner", [1, 128], BF16)

    z1b = z1a_t[:, :].bitcast(BF16)
    z2b = z2_t[:, :].bitcast(BF16)
    hb = h_t[:, :].bitcast(BF16)
    sA = sA_t[:, :].bitcast(BF16)
    sB = sB_t[:, :].bitcast(BF16)
    gact = g0_t[:, :].bitcast(BF16)
    xb = xb_t[:, :].bitcast(BF16)
    s1b = s1b_t[:, :].bitcast(BF16)
    s2b = s2b_t[:, :].bitcast(BF16)
    bcd = bcd_t[:, :].bitcast(BF16)
    buxb = bu_t[:, 0:DC * TB // 2].bitcast(BF16)
    xbL = [xb, buxb]

    stage = h_t
    sgnv = bu_t[:, :].bitcast(BF16)
    absb = xb

    SM = {}
    _off = [0]
    for nm, w in [("a", DC), ("A512", DC), ("A1024", DC), ("A1536", DC), ("A2048", DC),
                  ("P2", DC), ("P3", DC), ("hin", DC), ("hcoef", DC), ("tmp8", DC),
                  ("stcol", DC), ("hsc", NB * DC), ("aw2", DC), ("onef", 1)]:
        SM[nm] = small_t[:, _off[0]:_off[0] + w]
        _off[0] += w

    ps_a = es.enter_context(nc.psum_tensor("ps_a", [128, TB], F32))
    ps_b = es.enter_context(nc.psum_tensor("ps_b", [128, TB], F32))
    ps_c = es.enter_context(nc.psum_tensor("ps_c", [128, TB], F32))
    ps_dd = es.enter_context(nc.psum_tensor("ps_d", [128, TB], F32))
    ps_s1 = es.enter_context(nc.psum_tensor("ps_s1", [128, TB], F32))
    ps_s2 = es.enter_context(nc.psum_tensor("ps_s2", [128, TB], F32))
    ps_rb = es.enter_context(nc.psum_tensor("ps_rb", [128, TB], F32))
    ps_qb = es.enter_context(nc.psum_tensor("ps_qb", [128, TB], F32))
    psQ = [ps_a, ps_b, ps_c, ps_dd]
    psALL = [ps_a, ps_b, ps_c, ps_dd, ps_s1, ps_s2, ps_rb, ps_qb]

    sems = {}
    DMA_SEMS = ["yin", "ysg0", "ysg1", "ywsa", "ywsb", "ws1a", "ws1b", "ws2a", "ws2b",
                "wbca", "wbcb", "wbda", "wbdb", "yzsz", "yzsx", "yzlz", "yzlx", "yout",
                "gd", "wal"]
    TOT_SEMS = {"yin", "yout"}
    for nm in ["t", "s", "v", "cc"] + DMA_SEMS:
        sems[nm] = es.enter_context(nc.semaphore(nm))

    cnt = {k: 0 for k in sems}
    emits = {k: [] for k in ["t", "s", "v", "g", "y"]}

    def op(eng, fn, inc=None, waits=()):
        ws = {}
        for tok in waits:
            if tok is None:
                continue
            sn, sv = tok
            if sn in ws and (ws[sn] == -1 or (sv != -1 and ws[sn] >= sv)):
                continue
            ws[sn] = sv
        emits[eng].append((tuple(ws.items()), fn, inc))
        if inc is None:
            return None
        cnt[inc] += 16 if inc in DMA_SEMS else 1
        return (inc, cnt[inc])

    def dma(sem, outap, inap, waits=(), tot=None):
        eng = "g" if sem == "gd" else "y"
        q = nc.gpsimd if sem == "gd" else nc.sync
        tok = op(eng, lambda o=outap, i=inap, q=q: q.dma_start(out=o, in_=i),
                 inc=sem, waits=waits)
        if tot is None:
            tot = sem in TOT_SEMS
        return (sem, -1) if tot else tok

    def vop(fn, waits=()):
        return op("v", fn, inc="v", waits=waits)

    def sop(fn, waits=()):
        return op("s", fn, inc="s", waits=waits)

    def top(fn, waits=()):
        return op("t", fn, inc="t", waits=waits)

    def top_sil(fn, waits=()):
        return op("t", fn, None, waits=waits)

    V, S, TE, G = nc.vector, nc.scalar, nc.tensor, nc.gpsimd

    last = {}

    def LU(key):
        return list(last.get(key, []))

    def setLU(key, toks):
        last[key] = [t for t in toks if t is not None]

    def addLU(key, toks):
        last[key] = LU(key) + [t for t in toks if t is not None]

    _psctr = [0]

    def nxt_ps():
        i = _psctr[0] % 4
        _psctr[0] += 1
        return i

    vop(lambda: V.memset(onec_t[:, :], 1.0))
    vop(lambda: V.memset(oner_t[:, :], 1.0))
    vop(lambda: V.memset(zz_t[:, :], 0.0))
    vop(lambda: V.memset(SM["stcol"], 0.0))
    vop(lambda: V.memset(SM["onef"], 1.0))

    x_load_toks = []
    for k in range(DC):
        x_load_toks.append(dma("yin", x_t[:, k * T:(k + 1) * T],
                               u_d[k * 128:(k + 1) * 128, :]))
    sv_toks = []
    for dst, src in [(b1_t[:, :], b1_d), (b2_t[:, :], b2_d), (gam_t[:, :], gam_d),
                     (bet_t[:, :], bet_d), (msk_t[:, :], msk_d), (hm_t[:, :], hm_d)]:
        sv_toks.append(dma("yin", dst, src))
    tok_ad = dma("yin", SM["tmp8"], ad_d)
    tok_h0 = dma("yin", SM["hcoef"], h0_d)

    t_e = sop(lambda: S.activation(out=SM["a"], in_=SM["tmp8"], func=AF.Exp),
              waits=[tok_ad])
    t_sp = sop(lambda: S.activation(out=SM["a"], in_=SM["a"], func=AF.Ln, bias=1.0),
               waits=[t_e])
    t_a = vop(lambda: V.tensor_scalar_mul(SM["a"], SM["a"], -1.0), waits=[t_sp])
    t_p = vop(lambda: V.tensor_copy(SM["A512"], SM["a"]), waits=[t_a])
    for _ in range(9):
        t_p = vop(lambda: V.tensor_tensor(out=SM["A512"], in0=SM["A512"],
                                          in1=SM["A512"], op=ALU.mult), waits=[t_p])
    t_1024 = vop(lambda: V.tensor_tensor(out=SM["A1024"], in0=SM["A512"], in1=SM["A512"],
                                         op=ALU.mult), waits=[t_p])
    t_1536 = vop(lambda: V.tensor_tensor(out=SM["A1536"], in0=SM["A1024"], in1=SM["A512"],
                                         op=ALU.mult), waits=[t_1024])
    t_2048 = vop(lambda: V.tensor_tensor(out=SM["A2048"], in0=SM["A1024"], in1=SM["A1024"],
                                         op=ALU.mult), waits=[t_1024])
    t_P2 = vop(lambda: V.tensor_tensor(out=SM["P2"], in0=SM["A2048"], in1=SM["A2048"],
                                       op=ALU.mult), waits=[t_2048])
    t_P3 = vop(lambda: V.tensor_tensor(out=SM["P3"], in0=SM["P2"], in1=SM["A2048"],
                                       op=ALU.mult), waits=[t_P2])

    wprep = {"s1": [], "s2": [], "bt": [], "ct": [], "dt": []}

    a1_mm_tok = [None] * 8
    abs_last = []
    for k in range(DC):
        q = k % 2
        st_sl = stage[:, q * 4096:(q + 1) * 4096]
        sg_sl = sgnv[:, q * 4096:(q + 1) * 4096]
        tl = dma("ysg%d" % q, st_sl, w1t_d[k * 128:(k + 1) * 128, :], waits=LU(("stage", q)))
        ts = sop(lambda o=sg_sl, i=st_sl: S.activation(out=o, in_=i, func=AF.Sign),
                 waits=[tl] + LU(("sgn", q)))
        tab = sop(lambda i=st_sl: S.activation(out=absb, in_=i, func=AF.Abs),
                  waits=[tl] + abs_last)
        setLU(("stage", q), [ts, tab])
        td = dma("ywsa" if q == 0 else "ywsb", s1_d[k].rearrange("c p j -> p c j"),
                 sg_sl.rearrange("p (c j) -> p c j", c=HC, j=128), waits=[ts])
        setLU(("sgn", q), [td])
        wprep["s1"].append(td)
        abs_last = []
        for j in range(8):
            tm = (lambda j=j, k=k: TE.matmul(
                psALL[j][0:1, :], onec_t[:, :], absb[:, j * TB:(j + 1) * TB],
                start=(k == 0), stop=(k == DC - 1)))
            if k == DC - 1:
                a1_mm_tok[j] = top(tm, waits=[tab])
                abs_last.append(a1_mm_tok[j])
            else:
                abs_last.append(top(tm, waits=[tab]))

    a1_st_toks = []
    rstage = [rows_t[0:1, 0:TB], rows_t[0:1, TB:2 * TB]]
    rst_lu = [[], []]
    for j in range(8):
        rq = j % 2
        cp = sop(lambda j=j, rq=rq: S.copy(out=rstage[rq], in_=psALL[j][0:1, :]),
                 waits=[a1_mm_tok[j]] + rst_lu[rq])
        td = dma("wal", aw1_d[j * TB:(j + 1) * TB].unsqueeze(0),
                 rstage[rq], waits=[cp])
        rst_lu[rq] = [td]
        a1_st_toks.append(td)
        setLU(("ps_alpha", j), [cp])
    a1_dma_toks = []
    for c in range(HC):
        a1_dma_toks.append(dma("wal", aw1_t[:, c:c + 1],
                               aw1_d[c * 128:(c + 1) * 128].unsqueeze(1),
                               waits=[a1_st_toks[c // 4]]))

    a2_mm_tok = [None] * 2
    abs_last2 = list(abs_last)
    for g in range(8):
        q = g % 2
        st_sl = stage[:, q * 4096:(q + 1) * 4096]
        sg_sl = sgnv[:, q * 4096:(q + 1) * 4096]
        tl = dma("ysg%d" % q, st_sl.rearrange("p (c j) -> p c j", c=4, j=D),
                 w2t_d[g * 512:(g + 1) * 512, :].rearrange("(c p) j -> p c j", c=4, p=128),
                 waits=LU(("stage", q)))
        ts = sop(lambda o=sg_sl, i=st_sl: S.activation(out=o, in_=i, func=AF.Sign),
                 waits=[tl] + LU(("sgn", q)))
        tab = sop(lambda i=st_sl: S.activation(out=absb, in_=i, func=AF.Abs),
                  waits=[tl] + abs_last2)
        setLU(("stage", q), [ts, tab])
        tds = []
        for i in range(4):
            k = g * 4 + i
            td = dma("ywsa" if q == 0 else "ywsb", s2_d[k].rearrange("c p j -> p c j"),
                     sg_sl[:, i * D:(i + 1) * D].rearrange("p (c j) -> p c j", c=DC, j=128),
                     waits=[ts])
            tds.append(td)
            wprep["s2"].append(td)
        setLU(("sgn", q), tds)
        abs_last2 = []
        for i in range(4):
            for j in range(2):
                first = (g == 0 and i == 0)
                lastm = (g == 7 and i == 3)
                w = [tab] + (LU(("ps_alpha", j)) if first else [])
                tm = (lambda i=i, j=j, first=first, lastm=lastm: TE.matmul(
                    psALL[j][0:1, :], onec_t[:, :],
                    absb[:, i * D + j * TB:i * D + (j + 1) * TB],
                    start=first, stop=lastm))
                if lastm:
                    a2_mm_tok[j] = top(tm, waits=w)
                    abs_last2.append(a2_mm_tok[j])
                else:
                    abs_last2.append(top(tm, waits=w))

    a2_st_toks = []
    for j in range(2):
        rq = j % 2
        cp = sop(lambda j=j, rq=rq: S.copy(out=rstage[rq], in_=psALL[j][0:1, :]),
                 waits=[a2_mm_tok[j]] + rst_lu[rq])
        td = dma("wal", aw2_d[j * TB:(j + 1) * TB].unsqueeze(0),
                 rstage[rq], waits=[cp])
        rst_lu[rq] = [td]
        a2_st_toks.append(td)
        setLU(("ps_alpha2", j), [cp])
    a2_dma_toks = []
    for c in range(DC):
        a2_dma_toks.append(dma("wal", SM["aw2"][:, c:c + 1],
                               aw2_d[c * 128:(c + 1) * 128].unsqueeze(1),
                               waits=[a2_st_toks[c // 4]]))
    setLU("rows", rst_lu[0] + rst_lu[1])

    for name, src_d, dst_d in [("bt", bt_d, btb_d), ("ct", ct_d, ctb_d),
                               ("dt", dt_d, dtb_d)]:
        for k in range(DC):
            q = k % 2
            st_sl = stage[:, q * 4096:q * 4096 + D]
            sg_sl = sgnv[:, q * 4096:q * 4096 + D]
            tl = dma("ysg%d" % q, st_sl, src_d[k * 128:(k + 1) * 128, :], waits=LU(("stage", q)))
            tv = vop(lambda o=sg_sl, i=st_sl: V.tensor_copy(o, i),
                     waits=[tl] + LU(("sgn", q)))
            setLU(("stage", q), [tv])
            td = dma("ywsa" if q == 0 else "ywsb", dst_d[k].rearrange("c p j -> p c j"),
                     sg_sl.rearrange("p (c j) -> p c j", c=DC, j=128), waits=[tv])
            setLU(("sgn", q), [td])
            wprep[name].append(td)

    t_aw1 = vop(lambda: V.tensor_scalar_mul(aw1_t[:, :], aw1_t[:, :], 1.0 / D),
                waits=a1_dma_toks)
    t_aw2 = vop(lambda: V.tensor_scalar_mul(SM["aw2"], SM["aw2"], 0.5 / H),
                waits=a2_dma_toks)
    t_b2h = vop(lambda: V.tensor_scalar_mul(b2_t[:, :], b2_t[:, :], 0.5), waits=sv_toks)

    setLU("hb_init", LU(("stage", 0)) + LU(("stage", 1)))
    setLU("bu_init", LU(("sgn", 0)) + LU(("sgn", 1)))
    setLU(("xbuf", 0), abs_last2 + a2_mm_tok)
    for j in range(8):
        if j < 4:
            setLU(("ps", j), LU(("ps_alpha", j)) + LU(("ps_alpha2", j)))
    setLU("ps_s", LU(("ps_alpha", 4)) + LU(("ps_alpha", 5)))
    setLU("ps_rb", LU(("ps_alpha", 6)))
    setLU("ps_qb", LU(("ps_alpha", 7)))

    ROW = {nm: rows_t[0:1, i * TB:(i + 1) * TB] for i, nm in
           enumerate(["m", "v", "t", "r", "q"])}
    RBF = {nm: rbf_t[0:1, i * TB:(i + 1) * TB] for i, nm in
           enumerate(["rhi", "rlo", "qhi", "qlo"])}

    def ln_stats(src_slices, src_is_bf, chunk_waits, pre=None, alt=False):
        if pre is not None:
            tA, tB = pre
            movA = [sA[:, k * TB:(k + 1) * TB] for k in range(DC)]
        else:
            tA, tB = [], []
            for k in range(DC):
                sl = src_slices[k]
                if src_is_bf:
                    tA.append(None)
                else:
                    tA.append(sop(lambda o=sA[:, k * TB:(k + 1) * TB], i=sl:
                                  S.copy(out=o, in_=i),
                                  waits=chunk_waits[k] + LU(("sAc", k))))
                tB.append(vop(lambda o=sB[:, k * TB:(k + 1) * TB], i=sl:
                              V.tensor_tensor(out=o, in0=i, in1=i, op=ALU.mult),
                              waits=chunk_waits[k] + LU(("sBc", k))))
            movA = (src_slices if src_is_bf
                    else [sA[:, k * TB:(k + 1) * TB] for k in range(DC)])
        tok_s1 = tok_s2 = None
        for k in range(DC):
            w1 = ([tA[k]] if tA and tA[k] is not None else list(chunk_waits[k]))
            if k == 0:
                w1 = w1 + LU("ps_s")
            f1 = (lambda k=k, s=movA[k]: TE.matmul(ps_s1[0:1, :], onec_t[:, :], s,
                                                   start=(k == 0), stop=(k == DC - 1)))
            if k == DC - 1:
                tok_s1 = top(f1, waits=w1)
            else:
                top_sil(f1, waits=w1)
            w2 = [tB[k]] if tB else list(chunk_waits[k])
            f2 = (lambda k=k, s=sB[:, k * TB:(k + 1) * TB]: TE.matmul(
                ps_s2[0:1, :], onec_t[:, :], s, start=(k == 0), stop=(k == DC - 1)))
            if k == DC - 1:
                tok_s2 = top(f2, waits=w2)
            else:
                top_sil(f2, waits=w2)
        for k in range(DC):
            setLU(("sAc", k), [tok_s1])
            setLU(("sBc", k), [tok_s2])
        wr = LU("rows")
        t1 = vop(lambda: V.tensor_scalar_mul(ROW["m"], ps_s1[0:1, :], 1.0 / D),
                 waits=[tok_s1] + wr)
        t2 = vop(lambda: V.tensor_scalar(out=ROW["v"], in0=ps_s2[0:1, :], scalar1=1.0 / D,
                                         scalar2=EPS, op0=ALU.mult, op1=ALU.add),
                 waits=[tok_s2] + wr)
        vop(lambda: V.tensor_tensor(out=ROW["t"], in0=ROW["m"], in1=ROW["m"],
                                    op=ALU.mult), waits=[t1])
        t4 = vop(lambda: V.tensor_tensor(out=ROW["v"], in0=ROW["v"], in1=ROW["t"],
                                         op=ALU.subtract), waits=[t2])
        setLU("ps_s", [t1, t2])
        t5a = vop(lambda: V.reciprocal(out=ROW["t"], in_=ROW["v"]), waits=[t4])
        t5 = sop(lambda: S.activation(out=ROW["r"], in_=ROW["t"], func=AF.Sqrt),
                 waits=[t5a])
        t6 = vop(lambda: V.tensor_tensor(out=ROW["t"], in0=ROW["r"], in1=ROW["r"],
                                         op=ALU.mult), waits=[t5])
        vop(lambda: V.tensor_tensor(out=ROW["t"], in0=ROW["t"], in1=ROW["v"],
                                    op=ALU.mult), waits=[t6])
        vop(lambda: V.tensor_scalar(out=ROW["t"], in0=ROW["t"], scalar1=-0.5,
                                    scalar2=1.5, op0=ALU.mult, op1=ALU.add))
        t9 = vop(lambda: V.tensor_tensor(out=ROW["r"], in0=ROW["r"], in1=ROW["t"],
                                         op=ALU.mult))
        tq = vop(lambda: V.scalar_tensor_tensor(out=ROW["q"], in0=ROW["m"], scalar=-1.0,
                                                in1=ROW["r"], op0=ALU.mult, op1=ALU.mult),
                 waits=[t9])
        wb = LU("rbf")
        c1 = vop(lambda: V.tensor_copy(RBF["rhi"], ROW["r"]), waits=[t9] + wb)
        c2 = vop(lambda: V.tensor_tensor(out=RBF["rlo"], in0=ROW["r"], in1=RBF["rhi"],
                                         op=ALU.subtract), waits=[c1])
        c3 = vop(lambda: V.tensor_copy(RBF["qhi"], ROW["q"]), waits=[tq] + wb)
        c4 = vop(lambda: V.tensor_tensor(out=RBF["qlo"], in0=ROW["q"], in1=RBF["qhi"],
                                         op=ALU.subtract), waits=[c3])
        setLU("rows", [c2, c4])
        if alt:
            pbr, pbq, lur, luq = ps_s1, ps_s2, "ps_s", "ps_s"
        else:
            pbr, pbq, lur, luq = ps_rb, ps_qb, "ps_rb", "ps_qb"
        top_sil(lambda: TE.matmul(pbr[:, :], oner_t[:, :], RBF["rhi"], start=True,
                                  stop=False), waits=[c2, t1, t2] + LU(lur))
        tok_rb = top(lambda: TE.matmul(pbr[:, :], oner_t[:, :], RBF["rlo"], start=False,
                                       stop=True))
        top_sil(lambda: TE.matmul(pbq[:, :], oner_t[:, :], RBF["qhi"], start=True,
                                  stop=False), waits=[c4] + LU(luq))
        tok_qb = top(lambda: TE.matmul(pbq[:, :], oner_t[:, :], RBF["qlo"], start=False,
                                       stop=True))
        setLU("rbf", [tok_rb, tok_qb])
        if alt:
            setLU("ps_s", [tok_rb, tok_qb])
        return tok_rb, tok_qb, pbr, pbq, lur, luq

    def ln_to_z(src_slices, zdst, zkey, lnres, chunk_waits):
        tok_rb, tok_qb, pbr, pbq, lur, luq = lnres
        z_toks = []
        for k in range(DC):
            t1 = vop(lambda k=k, s=src_slices[k]: V.tensor_tensor(
                out=scr_t[:, (k % 2) * TB:(k % 2 + 1) * TB], in0=s, in1=pbr[:, :],
                op=ALU.mult),
                waits=[tok_rb] + chunk_waits[k] + LU(("scr", k % 2)))
            t2 = vop(lambda k=k: V.tensor_tensor(
                out=zdst[:, k * TB:(k + 1) * TB],
                in0=scr_t[:, (k % 2) * TB:(k % 2 + 1) * TB],
                in1=pbq[:, :], op=ALU.add), waits=[tok_qb] + LU((zkey, k)))
            setLU(("scr", k % 2), [t2])
            setLU((zkey, k), [t2])
            z_toks.append(t2)
        addLU(lur, z_toks)
        addLU(luq, z_toks)
        return z_toks

    def ffn_block(b, z_toks, zsrc, zkey, emit_ln_casts):
        gelu_toks = []
        z_read = None
        for c in range(HC):
            q4 = c % 2
            buf = s1b[:, q4 * DC * 128:(q4 + 1) * DC * 128]
            tl = dma("ws1a" if q4 == 0 else "ws1b",
                     buf.rearrange("p (k j) -> p k j", k=DC, j=128),
                     s1_d[:, c].rearrange("k p j -> p k j"),
                     waits=LU(("s1b", q4)) + (wprep["s1"] if c < 2 else []))
            psi = nxt_ps()
            mm_w = [tl] + LU(("ps", psi)) + (z_toks if c == 0 else [])
            tok_mm = None
            for k in range(DC):
                f = (lambda k=k, psi=psi, buf=buf: TE.matmul(
                    psQ[psi][:, :], buf[:, k * 128:(k + 1) * 128],
                    zsrc[:, k * TB:(k + 1) * TB], start=(k == 0), stop=(k == DC - 1)))
                if k == DC - 1:
                    tok_mm = top(f)
                else:
                    top_sil(f, waits=mm_w if k == 0 else ())
            setLU(("s1b", q4), [tok_mm])
            z_read = tok_mm
            tg = sop(lambda c=c, psi=psi: S.activation(
                out=hb[:, c * TB:(c + 1) * TB], in_=psQ[psi][:, :], func=AF.Gelu,
                scale=aw1_t[:, c:c + 1], bias=b1_t[:, c:c + 1]),
                waits=[tok_mm, t_aw1] + sv_toks + (LU("hb_init") if c == 0 else []))
            gelu_toks.append(tg)
            setLU(("ps", psi), [tg])
        setLU("hb_init", [])
        for k in range(DC):
            addLU((zkey, k), [z_read])
        x_toks, cA, cB = [], [], []
        for c in range(DC):
            psi = nxt_ps()
            tok_mm = None
            for half in range(2):
                hq = (2 * c + half) % 2
                buf = s2b[:, hq * 16 * 128:(hq + 1) * 16 * 128]
                tl = dma("ws2a" if hq == 0 else "ws2b",
                         buf.rearrange("p (k j) -> p k j", k=16, j=128),
                         s2_d[half * 16:(half + 1) * 16, c].rearrange("k p j -> p k j"),
                         waits=LU(("s2b", hq)) + (wprep["s2"] if c < 1 else []))
                mm_w = [tl] + ((LU(("ps", psi)) + gelu_toks) if half == 0 else [])
                for k16 in range(16):
                    k = half * 16 + k16
                    f = (lambda k=k, k16=k16, psi=psi, buf=buf: TE.matmul(
                        psQ[psi][:, :], buf[:, k16 * 128:(k16 + 1) * 128],
                        hb[:, k * TB:(k + 1) * TB], start=(k == 0), stop=(k == HC - 1)))
                    if k16 == 15:
                        tok_mm = top(f)
                    else:
                        top_sil(f, waits=mm_w if k16 == 0 else ())
                setLU(("s2b", hq), [tok_mm])
            te = sop(lambda c=c, psi=psi: S.activation(
                out=scr_t[:, (c % 2) * TB:(c % 2 + 1) * TB], in_=psQ[psi][:, :],
                func=AF.Identity,
                scale=SM["aw2"][:, c:c + 1], bias=b2_t[:, c:c + 1]),
                waits=[tok_mm] + LU(("scr", c % 2)) + [t_aw2, t_b2h])
            xsl = x_t[:, c * T + b * TB:c * T + (b + 1) * TB]
            tx = vop(lambda c=c, xsl=xsl: V.tensor_tensor(
                out=xsl, in0=xsl,
                in1=scr_t[:, (c % 2) * TB:(c % 2 + 1) * TB], op=ALU.add), waits=[te])
            setLU(("scr", c % 2), [tx])
            setLU(("ps", psi), [te])
            x_toks.append(tx)
            if emit_ln_casts:
                cA.append(sop(lambda c=c, xsl=xsl: S.copy(
                    out=sA[:, c * TB:(c + 1) * TB], in_=xsl),
                    waits=[tx] + LU(("sAc", c))))
                cB.append(vop(lambda c=c, xsl=xsl: V.tensor_tensor(
                    out=sB[:, c * TB:(c + 1) * TB], in0=xsl, in1=xsl, op=ALU.mult),
                    waits=[tx] + LU(("sBc", c))))
        return x_toks, cA, cB

    def b_stage(b, z2_toks):
        bu_toks = []
        for c in range(DC):
            q = c % 2
            buf = bcd[:, q * DC * 128:(q + 1) * DC * 128]
            tl = dma("wbca" if q == 0 else "wbcb",
                     buf.rearrange("p (k j) -> p k j", k=DC, j=128),
                     btb_d[:, c].rearrange("k p j -> p k j"),
                     waits=LU(("bcd", q)) + (wprep["bt"] if (b == 0 and c < 2) else []))
            psi = nxt_ps()
            mm_w = [tl] + LU(("ps", psi)) + (z2_toks if c == 0 else [])
            tok_mm = None
            for k in range(DC):
                f = (lambda k=k, psi=psi, buf=buf: TE.matmul(
                    psQ[psi][:, :], buf[:, k * 128:(k + 1) * 128],
                    z2b[:, k * TB:(k + 1) * TB], start=(k == 0), stop=(k == DC - 1)))
                if k == DC - 1:
                    tok_mm = top(f)
                else:
                    top_sil(f, waits=mm_w if k == 0 else ())
            setLU(("bcd", q), [tok_mm])
            addLU("z2use", [tok_mm])
            tb_ = sop(lambda c=c, psi=psi: S.copy(out=bu_t[:, c * TB:(c + 1) * TB],
                                                  in_=psQ[psi][:, :]),
                      waits=[tok_mm] + LU(("bu", c)) + (LU("bu_init") if (b == 0 and c == 0) else []))
            setLU(("ps", psi), [tb_])
            bu_toks.append(tb_)
        return bu_toks

    stcol_tok = {k: None for k in range(DC)}
    xs_store_toks = []

    def scan_stage(b, bu_toks):
        tcb_toks = []
        for k in range(DC):
            tsc = vop(lambda k=k: V.tensor_tensor_scan(
                out=xs_t[:, :],
                data0=SM["a"][:, k:k + 1].broadcast_to((128, TB)),
                data1=bu_t[:, k * TB:(k + 1) * TB],
                initial=SM["stcol"][:, k:k + 1],
                op0=ALU.mult, op1=ALU.add),
                waits=[bu_toks[k], t_a, stcol_tok[k]] + LU("xs"))
            setLU(("bu", k), [tsc])
            tst = vop(lambda k=k: V.tensor_copy(
                SM["stcol"][:, k:k + 1], xs_t[:, TB - 1:TB]),
                waits=[tsc])
            stcol_tok[k] = tst
            tcb = vop(lambda k=k: V.tensor_copy(xb[:, k * TB:(k + 1) * TB],
                                                xs_t[:, :]),
                      waits=[tsc] + (LU(("xbuf", 0)) if k == 0 else []))
            setLU("xs", [tcb])
            tcb_toks.append(tcb)
        txs = dma("yzsx", xs_d[b], xb[:, :], waits=tcb_toks)
        setLU(("xbuf", 0), [txs])
        xs_store_toks.append(txs)
        return tcb_toks

    def xsl_blk(b):
        return [x_t[:, k * T + b * TB:k * T + (b + 1) * TB] for k in range(DC)]

    ch_w_in = [[x_load_toks[k]] for k in range(DC)]
    ln1 = ln_stats(xsl_blk(0), False, ch_w_in)
    z1_toks = ln_to_z(xsl_blk(0), z1b, "z1k", ln1, ch_w_in)

    for b in range(NB):
        x_toks, cA, cB = ffn_block(b, z1_toks, z1b, "z1k", True)
        ln2 = ln_stats(None, False, [[t] for t in x_toks], pre=(cA, cB))
        ch_w = [[x_toks[k]] for k in range(DC)]
        z2_toks = ln_to_z(xsl_blk(b), z2b, "z2k", ln2, ch_w)
        addLU("z2use", z2_toks)
        if b + 1 < NB:
            chn = [[x_load_toks[k]] for k in range(DC)]
            ln1 = ln_stats(xsl_blk(b + 1), False, chn)
            z1_toks = ln_to_z(xsl_blk(b + 1), z1b, "z1k", ln1, chn)
        bu_toks = b_stage(b, z2_toks)
        for k in range(DC):
            setLU(("z2k", k), LU("z2use"))
        setLU("z2use", [])
        scan_stage(b, bu_toks)

    z2w = {}
    no_w = [[] for _ in range(DC)]
    ln2p = ln_stats(xsl_blk(0), False, no_w, alt=True)
    z2w[0] = ln_to_z(xsl_blk(0), z2b, "z2k", ln2p, no_w)

    cc_dmas = []
    for k in range(DC):
        cc_dmas.append(dma("gd", cc_in.ap()[k * 128:(k + 1) * 128].unsqueeze(1),
                           SM["stcol"][:, k:k + 1],
                           waits=[stcol_tok[kk] for kk in range(DC)], tot=False))
    tok_cc = op("g", lambda: G.collective_compute(
        "AllGather", ALU.bypass, replica_groups=[list(range(NCORES))],
        ins=[cc_in.ap().opt()], outs=[cc_out.ap().opt()]), inc="cc", waits=cc_dmas)
    fg_toks = []
    for k in range(DC):
        for r in range(NCORES):
            fg_toks.append(dma("gd", fg_t[:, k * 8 + r:k * 8 + r + 1],
                               cc_out.ap()[r, k * 128:(k + 1) * 128].unsqueeze(1),
                               waits=[tok_cc], tot=True))

    t_h0mv = vop(lambda: V.tensor_copy(SM["hsc"][:, 0:DC], SM["hcoef"]), waits=[tok_h0])
    h0_ap = SM["hsc"][:, 0:DC]
    fsel_aps = [SM["tmp8"], SM["hcoef"], SM["A1024"]]
    fsel_scr = bu_t[:, 4 * TB:4 * TB + 192]
    scan3_reads = [stcol_tok[k] for k in range(DC)] + LU("xs")
    fsel_toks = []
    for i in range(3):
        tm = vop(lambda i=i: V.tensor_tensor(
            out=fsel_scr[:, i * 64:(i + 1) * 64], in0=fg_t[:, :],
            in1=msk_t[:, i * 64:(i + 1) * 64], op=ALU.mult),
            waits=fg_toks + sv_toks + scan3_reads)
        tr = vop(lambda i=i, d=fsel_aps[i]: V.tensor_reduce(
            out=d, in_=fsel_scr[:, i * 64:(i + 1) * 64]
            .rearrange("p (k r) -> p k r", k=8, r=8),
            axis=AX.X, op=ALU.add), waits=[tm])
        fsel_toks.append(tr)
    vop(lambda: V.tensor_copy(SM["hin"], fsel_aps[0]), waits=fsel_toks)
    vop(lambda: V.scalar_tensor_tensor(out=SM["A1536"], in0=fsel_aps[1], scalar=1.0,
                                       in1=SM["A2048"], op0=ALU.mult, op1=ALU.mult))
    vop(lambda: V.tensor_tensor(out=SM["hin"], in0=SM["hin"], in1=SM["A1536"],
                                op=ALU.add))
    vop(lambda: V.tensor_tensor(out=SM["A1536"], in0=fsel_aps[2], in1=SM["P2"],
                                op=ALU.mult))
    vop(lambda: V.tensor_tensor(out=SM["hin"], in0=SM["hin"], in1=SM["A1536"],
                                op=ALU.add))
    vop(lambda: V.tensor_copy(SM["A1536"], hm_t[:, 0:8]))
    vop(lambda: V.tensor_tensor(out=SM["P2"], in0=SM["A2048"], in1=hm_t[:, 8:16],
                                op=ALU.mult))
    vop(lambda: V.tensor_tensor(out=SM["A1536"], in0=SM["A1536"], in1=SM["P2"],
                                op=ALU.add))
    vop(lambda: V.tensor_tensor(out=SM["P2"], in0=SM["A2048"], in1=SM["A2048"],
                                op=ALU.mult))
    vop(lambda: V.tensor_tensor(out=SM["tmp8"], in0=SM["P2"], in1=hm_t[:, 16:24],
                                op=ALU.mult))
    vop(lambda: V.tensor_tensor(out=SM["A1536"], in0=SM["A1536"], in1=SM["tmp8"],
                                op=ALU.add))
    vop(lambda: V.tensor_tensor(out=SM["tmp8"], in0=SM["P3"], in1=hm_t[:, 24:32],
                                op=ALU.mult), waits=[t_P3])
    vop(lambda: V.tensor_tensor(out=SM["A1536"], in0=SM["A1536"], in1=SM["tmp8"],
                                op=ALU.add))
    vop(lambda: V.tensor_tensor(out=SM["tmp8"], in0=SM["A1536"], in1=h0_ap,
                                op=ALU.mult), waits=[t_h0mv])
    th_in = vop(lambda: V.tensor_tensor(out=SM["hin"], in0=SM["hin"], in1=SM["tmp8"],
                                        op=ALU.add))
    ts0 = vop(lambda: V.tensor_copy(SM["hsc"][:, 0:8], SM["hin"]), waits=[th_in])
    ts1 = vop(lambda: V.tensor_tensor(out=SM["hsc"][:, 8:16], in0=SM["hin"],
                                      in1=SM["A512"], op=ALU.mult))
    ts2 = vop(lambda: V.tensor_tensor(out=SM["hsc"][:, 16:24], in0=SM["hsc"][:, 8:16],
                                      in1=SM["A512"], op=ALU.mult))
    ts3 = vop(lambda: V.tensor_tensor(out=SM["hsc"][:, 24:32], in0=SM["hsc"][:, 16:24],
                                      in1=SM["A512"], op=ALU.mult))
    hsc_toks = [ts0, ts1, ts2, ts3]

    setLU(("xbuf", 1), [stcol_tok[k] for k in range(4)] + fsel_toks + LU("xs"))

    tlx = [None] * NB
    corr_toks = [None] * NB

    def emit_loads(b):
        tlx[b] = dma("yzlx", xbL[b % 2][:, :], xs_d[b],
                     waits=[xs_store_toks[b]] + LU(("xbuf", b % 2)))
        setLU(("xbuf", b % 2), [tlx[b]])

    def emit_corr(b):
        toks = []
        for k in range(DC):
            ta = vop(lambda k=k: V.tensor_tensor_scan(
                out=apw_t[:, :], data0=SM["a"][:, k:k + 1].broadcast_to((128, TB)),
                data1=zz_t[:, :], initial=SM["onef"][:, 0:1],
                op0=ALU.mult, op1=ALU.add), waits=[t_a] + hsc_toks + LU("apw"))
            tc = vop(lambda k=k, b=b: V.scalar_tensor_tensor(
                out=xbL[b % 2][:, k * TB:(k + 1) * TB], in0=apw_t[:, :],
                scalar=SM["hsc"][:, b * 8 + k:b * 8 + k + 1],
                in1=xbL[b % 2][:, k * TB:(k + 1) * TB], op0=ALU.mult, op1=ALU.add),
                waits=[ta, tlx[b]])
            setLU("apw", [tc])
            addLU(("xbuf", b % 2), [tc])
            toks.append(tc)
        corr_toks[b] = toks

    cd_mm_toks = [[None] * DC for _ in range(NB)]
    cd_psi = [[None] * DC for _ in range(NB)]

    def cd_mms(b, crange):
        tok_mm = None
        for c in crange:
            q = c % 2
            bufC = bcd[:, q * DC * 128:(q + 1) * DC * 128]
            bufD = bcd[:, (2 + q) * DC * 128:(3 + q) * DC * 128]
            tlc = dma("wbca" if q == 0 else "wbcb",
                      bufC.rearrange("p (k j) -> p k j", k=DC, j=128),
                      ctb_d[:, c].rearrange("k p j -> p k j"),
                      waits=LU(("bcd", q)) + (wprep["ct"] if (b == 0 and c < 2) else []))
            tld = dma("wbda" if q == 0 else "wbdb",
                      bufD.rearrange("p (k j) -> p k j", k=DC, j=128),
                      dtb_d[:, c].rearrange("k p j -> p k j"),
                      waits=LU(("bcd", 2 + q)) + (wprep["dt"] if (b == 0 and c < 2) else []))
            psi = nxt_ps()
            mm_w = [tlc, tld] + LU(("ps", psi)) + \
                ((corr_toks[b] + z2w[b]) if c == crange[0] else [])
            for k in range(DC):
                f = (lambda k=k, psi=psi, bufC=bufC, b=b: TE.matmul(
                    psQ[psi][:, :], bufC[:, k * 128:(k + 1) * 128],
                    xbL[b % 2][:, k * TB:(k + 1) * TB], start=(k == 0), stop=False))
                top_sil(f, waits=mm_w if k == 0 else ())
            tok_mm = None
            for k in range(DC):
                f = (lambda k=k, psi=psi, bufD=bufD: TE.matmul(
                    psQ[psi][:, :], bufD[:, k * 128:(k + 1) * 128],
                    z2b[:, k * TB:(k + 1) * TB], start=False, stop=(k == DC - 1)))
                if k == DC - 1:
                    tok_mm = top(f)
                else:
                    top_sil(f)
            setLU(("bcd", q), [tok_mm])
            setLU(("bcd", 2 + q), [tok_mm])
            addLU(("xbuf", b % 2), [tok_mm])
            cd_mm_toks[b][c] = tok_mm
            cd_psi[b][c] = psi
        for k in range(DC):
            addLU(("z2k", k), [tok_mm])

    def cd_evac(b, crange, prev_t1=None):
        for c in crange:
            psi = cd_psi[b][c]
            w = [cd_mm_toks[b][c]] + LU(("gel", c))
            if prev_t1 is not None:
                w = w + [prev_t1[c]]
            tga = sop(lambda c=c, psi=psi: S.activation(
                out=gact[:, c * TB:(c + 1) * TB], in_=psQ[psi][:, :],
                func=AF.Gelu), waits=w)
            setLU(("ps", psi), [tga])
            setLU(("gel", c), [tga])
            gel_toks[b][c] = tga

    gel_toks = [[None] * DC for _ in range(NB)]

    emit_loads(0)
    emit_loads(1)
    emit_corr(0)
    cd_mms(0, list(range(0, 8)))
    cd_evac(0, list(range(0, 8)))
    ln2p = ln_stats(xsl_blk(1), False, no_w, alt=True)
    z2w[1] = ln_to_z(xsl_blk(1), z2b, "z2k", ln2p, no_w)
    emit_corr(1)

    fx = {k: [] for k in range(DC)}
    for b in range(NB):
        gsl = [gact[:, k * TB:(k + 1) * TB] for k in range(DC)]
        rb3, qb3, pbr3, pbq3, lur3, luq3 = ln_stats(
            gsl, True, [[gel_toks[b][k]] for k in range(DC)])
        if b + 2 < NB:
            emit_loads(b + 2)
        if b + 1 < NB:
            cd_mms(b + 1, list(range(0, 4)))
        res_toks = []
        rb_reads, qb_reads = [], []
        for k in range(DC):
            t1 = vop(lambda k=k, s=gsl[k]: V.tensor_tensor(
                out=scr_t[:, (k % 2) * TB:(k % 2 + 1) * TB], in0=s,
                in1=pbr3[:, :], op=ALU.mult),
                waits=[rb3, gel_toks[b][k]] + LU(("scr", k % 2)))
            t2 = vop(lambda k=k: V.tensor_tensor(
                out=scr_t[:, (k % 2) * TB:(k % 2 + 1) * TB],
                in0=scr_t[:, (k % 2) * TB:(k % 2 + 1) * TB],
                in1=pbq3[:, :], op=ALU.add), waits=[qb3])
            rb_reads.append(t1)
            qb_reads.append(t2)
            setLU(("gel", k), [t1, qb3])
            t3 = sop(lambda k=k: S.activation(
                out=scr_t[:, (k % 2) * TB:(k % 2 + 1) * TB],
                in_=scr_t[:, (k % 2) * TB:(k % 2 + 1) * TB],
                func=AF.Identity, scale=gam_t[:, k:k + 1], bias=bet_t[:, k:k + 1]),
                waits=[t2] + sv_toks)
            t4 = vop(lambda k=k: V.tensor_scalar(
                out=scr_t[:, (k % 2) * TB:(k % 2 + 1) * TB],
                in0=scr_t[:, (k % 2) * TB:(k % 2 + 1) * TB],
                scalar1=CLAMP, scalar2=-CLAMP, op0=ALU.min, op1=ALU.max), waits=[t3])
            xsl = x_t[:, k * T + b * TB:k * T + (b + 1) * TB]
            t5 = vop(lambda k=k, xsl=xsl: V.tensor_tensor(
                out=xsl, in0=xsl,
                in1=scr_t[:, (k % 2) * TB:(k % 2 + 1) * TB], op=ALU.add), waits=[t4])
            setLU(("scr", k % 2), [t5])
            res_toks.append(t5)
        setLU("ps_rb", rb_reads)
        setLU("ps_qb", qb_reads)
        if b + 1 < NB:
            cd_evac(b + 1, list(range(0, 4)), prev_t1=rb_reads)
        ln4 = ln_stats(xsl_blk(b), False, [[t] for t in res_toks])
        if b + 1 < NB:
            cd_mms(b + 1, list(range(4, 8)))
            cd_evac(b + 1, list(range(4, 8)), prev_t1=rb_reads)
        z4_toks = ln_to_z(xsl_blk(b), z1b, "z1k", ln4,
                          [[t] for t in res_toks])
        if b + 2 < NB:
            emit_corr(b + 2)
        x3_toks, _, _ = ffn_block(b, z4_toks, z1b, "z1k", False)
        for k in range(DC):
            fx[k].append(x3_toks[k])
        if b + 2 < NB:
            ln2p = ln_stats(xsl_blk(b + 2), False, no_w, alt=True)
            z2w[b + 2] = ln_to_z(xsl_blk(b + 2), z2b, "z2k", ln2p, no_w)

    st_toks = []
    for k in range(DC):
        st_toks.append(dma("yout", out_d[k * 128:(k + 1) * 128, :],
                           x_t[:, k * T:(k + 1) * T], waits=fx[k]))
    op("g", lambda: None, None, waits=st_toks)

    hi_water = {e: {} for e in emits}

    def replay(e, own):
        hw = hi_water[e]
        eng = {"y": nc.sync, "t": nc.tensor, "s": nc.scalar, "v": nc.vector,
               "g": nc.gpsimd}[e]
        kown = 0
        serialize = own in ("s", "v")
        for ws, fn, inc in emits[e]:
            if serialize and kown > 0 and hw.get(own, 0) < kown:
                eng.wait_ge(sems[own], kown)
                hw[own] = kown
            for sn, sv in ws:
                if sn == own and serialize:
                    continue
                if sn == own:
                    continue
                if sv == -1:
                    sv = cnt[sn]
                if hw.get(sn, 0) >= sv:
                    continue
                eng.wait_ge(sems[sn], sv)
                hw[sn] = sv
            ins = fn()
            if inc is not None:
                assert ins is not None, "op expected to return instruction"
                ins.then_inc(sems[inc], 16 if inc in DMA_SEMS else 1)
                if inc == own:
                    kown += 1

    with nc.Block() as block:
        @block.sync
        def _(sync):
            replay("y", None)

        @block.tensor
        def _(te):
            replay("t", "t")

        @block.scalar
        def _(sc):
            replay("s", "s")

        @block.vector
        def _(ve):
            replay("v", "v")

        @block.gpsimd
        def _(gp):
            replay("g", "g")

    es.close()
    return nc


_NC_CACHE = None


def kernel(L0_attn_u, L0_attn_A_diag, L0_attn_Bmat, L0_attn_Cmat, L0_attn_Dmat,
           L0_attn_h0, L0_attn_ln_gamma, L0_attn_ln_beta,
           L0_ffn_w1, L0_ffn_b1, L0_ffn_w2, L0_ffn_b2):
    global _NC_CACHE
    u = np.asarray(L0_attn_u, np.float32)
    w1 = np.ascontiguousarray(np.asarray(L0_ffn_w1, np.float32))
    w2 = np.ascontiguousarray(np.asarray(L0_ffn_w2, np.float32))
    Bm = np.asarray(L0_attn_Bmat, np.float32)
    Cm = np.asarray(L0_attn_Cmat, np.float32)
    Dm = np.asarray(L0_attn_Dmat, np.float32)

    def vr(v, kk):
        return np.ascontiguousarray(np.asarray(v, np.float32).reshape(kk, 128).T)

    def rep(vrow):
        return np.ascontiguousarray(
            np.broadcast_to(vrow[None, :], (128, vrow.shape[0])).astype(np.float32))

    shared = {
        "w1t": np.ascontiguousarray(w1.T), "w2t": np.ascontiguousarray(w2.T),
        "bt": np.ascontiguousarray(Bm.T), "ct": np.ascontiguousarray(Cm.T),
        "dt": np.ascontiguousarray(Dm.T),
        "b1r": vr(L0_ffn_b1, HC), "b2r": vr(L0_ffn_b2, DC),
        "adr": vr(L0_attn_A_diag, DC), "h0r": vr(L0_attn_h0, DC),
        "gamr": vr(L0_attn_ln_gamma, DC), "betr": vr(L0_attn_ln_beta, DC),
    }
    in_maps = []
    for c in range(NCORES):
        bidx, j = c // 4, c % 4
        base = 4 * bidx
        uc = np.ascontiguousarray(u[bidx, j * T:(j + 1) * T, :].T)
        msk = np.zeros(192, np.float32)
        for i in range(3):
            if j - 1 - i >= 0:
                r = base + j - 1 - i
                for k in range(DC):
                    msk[i * 64 + k * 8 + r] = 1.0
        hm = np.zeros(32, np.float32)
        hm[j * 8:(j + 1) * 8] = 1.0
        m = dict(shared)
        m["u"] = uc
        m["msk"] = rep(msk)
        m["hm"] = rep(hm)
        in_maps.append(m)

    if _NC_CACHE is None:
        _NC_CACHE = build_nc()
    res = run_bass_kernel_spmd(_NC_CACHE, in_maps, core_ids=list(range(NCORES)))

    out = np.empty((2, 8192, 1024), np.float32)
    for c in range(NCORES):
        bidx, j = c // 4, c % 4
        out[bidx, j * T:(j + 1) * T, :] = res.results[c]["out"].T
    return out


if __name__ == "__main__":
    import reference as _r
    ins = _r.setup_inputs(0)
    got = kernel(**{k: np.asarray(v) for k, v in ins.items()})
    exp = np.asarray(_r.reference(**ins))
    num = float(np.mean((got - exp) ** 2))
    den = float(np.mean(exp ** 2))
    print("res_var_ratio:", num / den, " max_abs:", float(np.abs(got - exp).max()))
